# version 13
# baseline (speedup 1.0000x reference)
import contextlib
import numpy as np
import ml_dtypes
import concourse.bass as bass
import concourse.mybir as mybir
from concourse.bass_utils import run_bass_kernel_spmd
from concourse.aot_env import allow_ici_all_to_all

F32 = mybir.dt.float32
BF16 = mybir.dt.bfloat16
AF = mybir.ActivationFunctionType
ALU = mybir.AluOpType
AX = mybir.AxisListType
BF = ml_dtypes.bfloat16

D = 2048
H = 16
S = 4096
TPC = 1024
RC = 12582912.0
ISQ = float(1.0 / np.sqrt(np.float64(128.0)))


def host_prep(f64=False):
    dt = np.float64 if f64 else np.float32
    c = {}
    c["ident"] = np.eye(128, dtype=BF)
    c["ident32"] = np.eye(128, dtype=np.float32)
    c["inv128"] = np.full((128, 1), 1.0 / 128.0, dtype=BF)
    c["onesrow"] = np.ones((1, 128), dtype=BF)
    kv = np.arange(128)[:, None]
    cc = np.arange(896)[None, :]
    c["mask"] = (kv + 384 <= cc).astype(BF)
    sel = np.zeros((8, 8 * 128), dtype=BF)
    for e in range(8):
        sel[e, e * 128:(e + 1) * 128] = 1.0
    c["sel"] = sel
    inv = (1.0 / np.power(dt(10000.0), np.arange(0, 128, 2).astype(dt) / dt(128.0)))
    t = np.arange(S).astype(dt)
    f = np.outer(t, inv)
    c["_cosT"] = np.ascontiguousarray(np.cos(f).T.astype(np.float32))
    c["_sinT"] = np.ascontiguousarray(np.sin(f).T.astype(np.float32))
    return c


def rope_rep_for_core(c, core):
    s0 = (core % 4) * TPC
    ra = np.concatenate([c["_cosT"][:, s0:s0 + TPC], c["_sinT"][:, s0:s0 + TPC]], axis=0)
    rb = np.concatenate([c["_sinT"][:, s0:s0 + TPC], c["_cosT"][:, s0:s0 + TPC]], axis=0)
    ra_rep = np.concatenate([np.tile(ra[:, g * 512:(g + 1) * 512], (1, 4)) for g in range(2)], axis=1)
    rb_rep = np.concatenate([np.tile(rb[:, g * 512:(g + 1) * 512], (1, 4)) for g in range(2)], axis=1)
    return ra_rep.astype(BF), rb_rep.astype(BF)


def build_mixer(layer):
    nc = bass.Bass("TRN2", target_bir_lowering=False)
    io = {}
    io["x_d"] = nc.dram_tensor("x", [TPC, D], F32, kind="ExternalInput")
    io["w_d"] = nc.dram_tensor("wqkv", [D, 3 * D], F32, kind="ExternalInput")
    io["ra_d"] = nc.dram_tensor("rope_a", [128, 4096], BF16, kind="ExternalInput")
    io["rb_d"] = nc.dram_tensor("rope_b", [128, 4096], BF16, kind="ExternalInput")
    io["id_d"] = nc.dram_tensor("ident32", [128, 128], F32, kind="ExternalInput")
    if layer == 0:
        io["bq_d"] = nc.dram_tensor("bqkv128", [128, 48], F32, kind="ExternalInput")
        io["bv_d"] = nc.dram_tensor("bvbc", [128, D], F32, kind="ExternalInput")
    else:
        io["iv_d"] = nc.dram_tensor("inv128", [128, 1], BF16, kind="ExternalInput")
        io["or_d"] = nc.dram_tensor("onesrow", [1, 128], BF16, kind="ExternalInput")
        io["ls_d"] = nc.dram_tensor("lnsc", [128, 1], F32, kind="ExternalInput")
        io["lb_d"] = nc.dram_tensor("lnbi", [128, 1], F32, kind="ExternalInput")
    io["qT_d"] = nc.dram_tensor("qT", [2048, TPC], BF16, kind="ExternalOutput")
    io["kT_d"] = nc.dram_tensor("kT", [2048, TPC], BF16, kind="ExternalOutput")
    io["v_d"] = nc.dram_tensor("vout", [8 * TPC, 256], BF16, kind="ExternalOutput")
    with contextlib.ExitStack() as ctx:
        xsb = ctx.enter_context(nc.sbuf_tensor("xsb", [128, 8 * D], F32))
        emit_mixer(nc, layer, xsb, io, ctx.enter_context, load_x=True)
    return nc


def emit_mixer(nc, layer, xsb, io, sement, load_x, coll=None):
    x_d = io.get("x_d")
    w_d = io["w_d"]
    ra_d, rb_d, id_d = io["ra_d"], io["rb_d"], io["id_d"]
    if layer == 0:
        bq_d, bv_d = io["bq_d"], io["bv_d"]
    else:
        iv_d, or_d, ls_d, lb_d = io["iv_d"], io["or_d"], io["ls_d"], io["lb_d"]
    qT_d, kT_d, v_d = io["qT_d"], io["kT_d"], io["v_d"]

    with contextlib.ExitStack() as ctx:
        ent = ctx.enter_context
        xT = ent(nc.sbuf_tensor("xT", [128, 16 * TPC], BF16))
        wb = ent(nc.sbuf_tensor("wb", [128, 2 * 16 * 512], BF16))
        tmp = ent(nc.sbuf_tensor("tmp", [128, 2 * 2048], BF16))
        qko = ent(nc.sbuf_tensor("qko", [128, 2 * 2048], BF16))
        rsc = ent(nc.sbuf_tensor("rsc", [128, 2 * 2048], BF16))
        ra = ent(nc.sbuf_tensor("ra", [128, 4096], BF16))
        rb = ent(nc.sbuf_tensor("rb", [128, 4096], BF16))
        ids32 = ent(nc.sbuf_tensor("ids32", [128, 128], F32))
        if layer == 0:
            bq = ent(nc.sbuf_tensor("bq", [128, 48], F32))
            bv = ent(nc.sbuf_tensor("bv", [128, D], F32))
        else:
            iv = ent(nc.sbuf_tensor("iv", [128, 1], BF16))
            onr = ent(nc.sbuf_tensor("onr", [1, 128], BF16))
            lns = ent(nc.sbuf_tensor("lns", [128, 1], F32))
            lnb = ent(nc.sbuf_tensor("lnb", [128, 1], F32))
            eps = ent(nc.sbuf_tensor("eps", [128, 1], F32))
            sqb = ent(nc.sbuf_tensor("sqb", [128, 512], BF16))
            stm = ent(nc.sbuf_tensor("stm", [1, 512], BF16))
            str_ = ent(nc.sbuf_tensor("str", [1, 512], BF16))
            msf = ent(nc.sbuf_tensor("msf", [1, 512], F32))
            vrf = ent(nc.sbuf_tensor("vrf", [1, 512], F32))
            lnv = ent(nc.sbuf_tensor("lnv", [1, 512], F32))
            t1b = ent(nc.sbuf_tensor("t1b", [128, 512], BF16))
        dX = sement(nc.semaphore("dX"))
        dW0 = sement(nc.semaphore("dW0"))
        dW1 = sement(nc.semaphore("dW1"))
        dS0 = sement(nc.semaphore("dS0"))
        dS1 = sement(nc.semaphore("dS1"))
        tMM = sement(nc.semaphore("tMM"))
        tEV = sement(nc.semaphore("tEV"))
        vv = sement(nc.semaphore("vv"))
        tH = sement(nc.semaphore("tH"))

        def rope_done(i):
            return 6 * (i + 1)

        def v_done(i):
            return 96 + 4 * (i - 15)

        def sdp(i):
            if i < 0:
                return 0
            return sum(2 for j in range(i % 2, i + 1, 2))

        npre = (5 if layer == 0 else 7) + (1 if load_x else 0)

        with nc.psum_tensor("psT1", [128, 128], F32) as psT1, \
             nc.psum_tensor("psT2", [128, 128], F32) as psT2, \
             nc.Block() as blk:
            @blk.sync
            def _(e):
                if load_x:
                    e.dma_start(out=xsb[:, :].rearrange("p (tt d) -> p tt d", d=D),
                                in_=x_d[:, :].rearrange("(tt p) d -> p tt d", p=128)).then_inc(dX, 16)
                e.dma_start(out=ra[:, :], in_=ra_d[:, :]).then_inc(dX, 16)
                e.dma_start(out=rb[:, :], in_=rb_d[:, :]).then_inc(dX, 16)
                e.dma_start(out=ids32[:, :], in_=id_d[:, :]).then_inc(dX, 16)
                if layer == 0:
                    e.dma_start(out=bq[:, :], in_=bq_d[:, :]).then_inc(dX, 16)
                else:
                    e.dma_start(out=iv[:, :], in_=iv_d[:, :]).then_inc(dX, 16)
                    e.dma_start(out=onr[:, :], in_=or_d[:, :]).then_inc(dX, 16)
                if layer == 0:
                    e.dma_start(out=bv[:, :], in_=bv_d[:, :]).then_inc(dX, 16)
                else:
                    e.dma_start(out=lns[:, :], in_=ls_d[:, :]).then_inc(dX, 16)
                    e.dma_start(out=lnb[:, :], in_=lb_d[:, :]).then_inc(dX, 16)

            @blk.gpsimd
            def _(e):
                for st in range(2):
                    src = w_d[:, st * 512:(st + 1) * 512].rearrange("(kc p) n -> p kc n", p=128)
                    e.dma_start(out=wb[:, st * 8192: st * 8192 + 8192].rearrange("p (kc n) -> p kc n", n=512),
                                in_=src).then_inc(dW0 if st % 2 == 0 else dW1, 16)

            @blk.tensor
            def _(e):
                e.wait_ge(dX, 16 * npre)
                for i in range(128):
                    tt, dc = divmod(i, 16)
                    ps = psT1 if i % 2 == 0 else psT2
                    if i >= 2:
                        e.wait_ge(tEV, i - 1)
                    e.transpose(ps[:, :], xsb[:, tt * D + dc * 128: tt * D + dc * 128 + 128],
                                ids32[:, :]).then_inc(tMM, 1)

            @blk.vector
            def _(e):
                if layer == 1:
                    e.memset(eps[:, :], 1e-5)
                for i in range(128):
                    tt, dc = divmod(i, 16)
                    ps = psT1 if i % 2 == 0 else psT2
                    e.wait_ge(tMM, i + 1)
                    e.tensor_copy(xT[:, dc * TPC + tt * 128: dc * TPC + tt * 128 + 128],
                                  ps[:, :]).then_inc(tEV, 1)

        T0, E0 = 128, 128
        HS = 13

        def hafter(i):
            return HS * 4 * (i + 1)

        def make_emit_hT(psH1, psH2, psBC):
            def emit_hT(e, idx):
                for cc in range(4):
                    b0 = HS * (4 * idx + cc)
                    q = qko[:, (idx % 2) * 2048 + cc * 512:(idx % 2) * 2048 + cc * 512 + 512]
                    e.wait_ge(tH, b0)
                    e.wait_ge(vv, rope_done(idx))
                    e.matmul(psH1[0:1, :], iv[:, :], q, start=True, stop=True).then_inc(tH, 1)
                    e.wait_ge(tH, b0 + 2)
                    e.matmul(psH2[0:1, :], iv[:, :], sqb[:, :], start=True, stop=True).then_inc(tH, 1)
                    e.wait_ge(tH, b0 + 8)
                    e.matmul(psBC[:, 0:512], onr[:, :], stm[0:1, :], start=True, stop=True).then_inc(tH, 1)
                    e.wait_ge(tH, b0 + 9)
                    e.matmul(psBC[:, 512:1024], onr[:, :], str_[0:1, :], start=True, stop=True).then_inc(tH, 1)
            return emit_hT

        psA = ent(nc.psum_tensor("psA", [128, 2048], F32))
        if layer == 0:
            psB = ent(nc.psum_tensor("psB", [128, 2048], F32))
        else:
            psB = None
            psH1 = ent(nc.psum_tensor("psH1", [128, 512], F32))
            psH2 = ent(nc.psum_tensor("psH2", [128, 512], F32))
            psBC = ent(nc.psum_tensor("psBC", [128, 1024], F32))
            emit_hT = make_emit_hT(psH1, psH2, psBC)
        with nc.Block() as blk:
            @blk.gpsimd
            def _(e):
                def wload(st):
                    e.wait_ge(tMM, T0 + 2 * st - 2)
                    src = w_d[:, st * 512:(st + 1) * 512].rearrange("(kc p) n -> p kc n", p=128)
                    e.dma_start(out=wb[:, (st % 2) * 8192:(st % 2) * 8192 + 8192].rearrange("p (kc n) -> p kc n", n=512),
                                in_=src).then_inc(dW0 if st % 2 == 0 else dW1, 16)

                def a2a(key_in, key_out):
                    with allow_ici_all_to_all():
                        e.collective_compute("AllToAll", ALU.bypass,
                                             replica_groups=[list(range(8))],
                                             ins=[coll[key_in].ap().opt()],
                                             outs=[coll[key_out].ap().opt()]).then_inc(coll["cc"], 1)

                for st in range(2, 8):
                    wload(st)
                if coll is not None:
                    e.wait_ge(dS0, 16 * sdp(6))
                    e.wait_ge(dS1, 16 * sdp(7))
                    a2a("q_in", "q_out")
                for st in range(8, 10):
                    wload(st)
                if coll is not None:
                    e.wait_ge(dS0, 16 * sdp(14))
                    e.wait_ge(dS1, 16 * sdp(15))
                    a2a("k_in", "k_out")
                for st in range(10, 12):
                    wload(st)
                if coll is not None:
                    e.wait_ge(dS0, 16 * sdp(22))
                    e.wait_ge(dS1, 16 * sdp(23))
                    a2a("v_in", "v_out")

            @blk.tensor
            def _(e):
                for st in range(12):
                    e.wait_ge(dW0 if st % 2 == 0 else dW1, 16 * (st // 2 + 1))
                    wv = 8192 * (st % 2)
                    for g in range(2):
                        idx = st * 2 + g
                        if layer == 0:
                            ps = psA if idx % 2 == 0 else psB
                        else:
                            ps = psA
                        if layer == 0:
                            if 2 <= idx < 18:
                                e.wait_ge(tEV, E0 + 4 * (idx - 1))
                            elif idx >= 18:
                                e.wait_ge(vv, v_done(idx - 2))
                        else:
                            if 1 <= idx <= 16:
                                e.wait_ge(tEV, E0 + 4 * idx)
                            elif idx > 16:
                                e.wait_ge(vv, v_done(idx - 1))
                            if idx == 16:
                                e.wait_ge(tH, hafter(15))
                        if st < 8:
                            for cc in range(4):
                                for kc in range(16):
                                    mm = e.matmul(ps[:, cc * 512:(cc + 1) * 512],
                                                  wb[:, wv + kc * 512 + cc * 128: wv + kc * 512 + cc * 128 + 128],
                                                  xT[:, kc * TPC + g * 512: kc * TPC + g * 512 + 512],
                                                  start=(kc == 0), stop=(kc == 15))
                                    if cc == 3 and kc == 15:
                                        mm.then_inc(tMM, 1)
                            if layer == 1:
                                if idx >= 1:
                                    emit_hT(e, idx - 1)
                                if idx == 15:
                                    emit_hT(e, 15)
                        else:
                            for tt in range(4):
                                tok = g * 512 + tt * 128
                                for kc in range(16):
                                    mm = e.matmul(ps[:, tt * 512:(tt + 1) * 512],
                                                  xT[:, kc * TPC + tok: kc * TPC + tok + 128],
                                                  wb[:, wv + kc * 512: wv + kc * 512 + 512],
                                                  start=(kc == 0), stop=(kc == 15))
                                    if tt == 3 and kc == 15:
                                        mm.then_inc(tMM, 1)

            @blk.scalar
            def _(e):
                def emit_evac(idx):
                    st_, g_ = idx // 2, idx % 2
                    ps = (psA if idx % 2 == 0 else psB) if layer == 0 else psA
                    e.wait_ge(tMM, T0 + idx + 1)
                    if idx >= 2:
                        e.wait_ge(vv, rope_done(idx - 2))
                    for cc in range(4):
                        dst = tmp[:, (idx % 2) * 2048 + cc * 512:(idx % 2) * 2048 + cc * 512 + 512]
                        if layer == 0:
                            e.activation(dst, ps[:, cc * 512:(cc + 1) * 512], AF.Identity,
                                         bias=bq[:, st_ * 4 + cc: st_ * 4 + cc + 1]).then_inc(tEV, 1)
                        else:
                            e.activation(dst, ps[:, cc * 512:(cc + 1) * 512],
                                         AF.Identity).then_inc(tEV, 1)

                def emit_chainA(idx):
                    for cc in range(4):
                        b0 = HS * (4 * idx + cc)
                        q = qko[:, (idx % 2) * 2048 + cc * 512:(idx % 2) * 2048 + cc * 512 + 512]
                        e.wait_ge(tH, b0 + 1)
                        e.wait_ge(vv, rope_done(idx))
                        e.activation(sqb[:, :], q, AF.Square).then_inc(tH, 1)
                        e.wait_ge(tH, b0 + 6)
                        e.activation(lnv[0:1, :], vrf[0:1, :], AF.Ln, bias=eps[0:1, :]).then_inc(tH, 1)
                        e.wait_ge(tH, b0 + 7)
                        e.activation(str_[0:1, :], lnv[0:1, :], AF.Exp, scale=-0.5).then_inc(tH, 1)

                if layer == 0:
                    for idx in range(16):
                        emit_evac(idx)
                else:
                    emit_evac(0)
                    for idx in range(1, 16):
                        emit_evac(idx)
                        emit_chainA(idx - 1)
                    emit_chainA(15)

            @blk.vector
            def _(e):
                nv = 0

                def vop(inst):
                    nonlocal nv
                    inst.then_inc(vv, 1)
                    nv += 1
                    return inst

                for st in range(12):
                    for g in range(2):
                        idx = st * 2 + g
                        o = (idx % 2) * 2048
                        if st < 8:
                            e.wait_ge(tEV, E0 + 4 * (idx + 1))
                            if idx >= 2:
                                e.wait_ge(dS0 if idx % 2 == 0 else dS1, 16 * sdp(idx - 2))
                                if layer == 1:
                                    e.wait_ge(tH, hafter(idx - 2))
                            gg = g * 2048
                            e.wait_ge(vv, nv)
                            vop(e.tensor_tensor(rsc[0:64, 0:2048], tmp[0:64, o:o + 2048],
                                                ra[0:64, gg:gg + 2048], ALU.mult))
                            e.wait_ge(vv, nv)
                            vop(e.tensor_tensor(rsc[0:64, 2048:4096], tmp[64:128, o:o + 2048],
                                                ra[64:128, gg:gg + 2048], ALU.mult))
                            e.wait_ge(vv, nv)
                            vop(e.tensor_tensor(rsc[64:128, 0:2048], tmp[0:64, o:o + 2048],
                                                rb[0:64, gg:gg + 2048], ALU.mult))
                            e.wait_ge(vv, nv)
                            vop(e.tensor_tensor(rsc[64:128, 2048:4096], tmp[64:128, o:o + 2048],
                                                rb[64:128, gg:gg + 2048], ALU.mult))
                            e.wait_ge(vv, nv)
                            vop(e.tensor_tensor(qko[0:64, o:o + 2048], rsc[0:64, 0:2048],
                                                rsc[0:64, 2048:4096], ALU.subtract))
                            e.wait_ge(vv, nv)
                            vop(e.tensor_tensor(qko[64:128, o:o + 2048], rsc[64:128, 0:2048],
                                                rsc[64:128, 2048:4096], ALU.add))
                            if layer == 1:
                                for cc in range(4):
                                    b0 = HS * (4 * idx + cc)
                                    q = qko[:, o + cc * 512: o + cc * 512 + 512]
                                    e.wait_ge(tH, b0 + 3)
                                    e.tensor_copy(stm[0:1, :], psH1[0:1, :]).then_inc(tH, 1)
                                    e.wait_ge(tH, b0 + 4)
                                    e.tensor_tensor(msf[0:1, :], stm[0:1, :], stm[0:1, :], ALU.mult).then_inc(tH, 1)
                                    e.wait_ge(tH, b0 + 5)
                                    e.tensor_tensor(vrf[0:1, :], psH2[0:1, :], msf[0:1, :], ALU.subtract).then_inc(tH, 1)
                                    e.wait_ge(tH, b0 + 10)
                                    e.tensor_tensor(t1b[:, :], q, psBC[:, 0:512], ALU.subtract).then_inc(tH, 1)
                                    e.wait_ge(tH, b0 + 11)
                                    e.tensor_tensor(t1b[:, :], t1b[:, :], psBC[:, 512:1024], ALU.mult).then_inc(tH, 1)
                                    e.wait_ge(tH, b0 + 12)
                                    e.tensor_scalar(q, t1b[:, :], lns[:, :], lnb[:, :],
                                                    ALU.mult, ALU.add).then_inc(tH, 1)
                        else:
                            ps = (psA if idx % 2 == 0 else psB) if layer == 0 else psA
                            e.wait_ge(tMM, T0 + idx + 1)
                            if idx >= 2:
                                e.wait_ge(dS0 if idx % 2 == 0 else dS1, 16 * sdp(idx - 2))
                            for tt in range(4):
                                dst = qko[:, o + tt * 512: o + tt * 512 + 512]
                                e.wait_ge(vv, nv)
                                if layer == 0:
                                    vop(e.tensor_tensor(dst, ps[:, tt * 512:(tt + 1) * 512],
                                                        bv[:, (st - 8) * 512:(st - 7) * 512], ALU.add))
                                else:
                                    vop(e.tensor_copy(dst, ps[:, tt * 512:(tt + 1) * 512]))

            @blk.sync
            def _(e):
                for st in range(12):
                    for g in range(2):
                        idx = st * 2 + g
                        o = (idx % 2) * 2048
                        if st < 8:
                            if layer == 0:
                                e.wait_ge(vv, rope_done(idx))
                            else:
                                e.wait_ge(tH, hafter(idx))
                            dst = (qT_d if st < 4 else kT_d)[:, :].rearrange(
                                "(a sh c p) t -> p a sh c t", sh=2, c=2, p=128)
                            for sh in range(2):
                                e.dma_start(out=dst[:, st % 4, sh, :, g * 512:(g + 1) * 512],
                                            in_=qko[:, o + sh * 1024:o + sh * 1024 + 1024]
                                            .rearrange("p (c t) -> p c t", t=512)
                                            ).then_inc(dS0 if idx % 2 == 0 else dS1, 16)
                        else:
                            e.wait_ge(vv, v_done(idx))
                            dst = v_d[:, :].rearrange("(j tt p) c -> p j tt c", j=8, p=128)
                            for jh in range(2):
                                e.dma_start(out=dst[:, (st - 8) * 2 + jh, g * 4:g * 4 + 4, :],
                                            in_=qko[:, o:o + 2048]
                                            .rearrange("p (tt jh c) -> p tt jh c", jh=2, c=256)[:, :, jh, :]
                                            ).then_inc(dS0 if idx % 2 == 0 else dS1, 16)
                e.wait_ge(dS0, 16 * sdp(22))
                e.wait_ge(dS1, 16 * sdp(23))


def attn_schedule():
    sched = []
    for qg in range(8):
        nkv = 4 * qg + 4
        tiles = [(kvt, 128 * (kvt - 4 * qg) if kvt >= 4 * qg else None) for kvt in range(nkv)]
        groups = [tiles[i:i + 3] for i in range(0, nkv, 3)]
        for kg, gt in enumerate(groups):
            sched.append((qg, kg, gt, len(groups)))
    return sched


def build_attn(layer):
    nc = bass.Bass("TRN2", target_bir_lowering=False)
    io = {}
    io["qa_d"] = nc.dram_tensor("qa", [2048, TPC], BF16, kind="ExternalInput")
    io["ka_d"] = nc.dram_tensor("ka", [2048, TPC], BF16, kind="ExternalInput")
    io["vg_d"] = nc.dram_tensor("vg", [8 * TPC, 256], BF16, kind="ExternalInput")
    io["mk_d"] = nc.dram_tensor("mask", [128, 896], BF16, kind="ExternalInput")
    io["id_d"] = nc.dram_tensor("ident32", [128, 128], F32, kind="ExternalInput")
    io["ctxa_d"] = nc.dram_tensor("ctxa", [2048, TPC], BF16, kind="ExternalOutput")
    with contextlib.ExitStack() as ctx:
        emit_attn(nc, layer, io, ctx.enter_context)
    return nc


def emit_attn(nc, layer, io, sement, coll=None):
    mk_d, id_d, ctxa_d = io["mk_d"], io["id_d"], io["ctxa_d"]

    sched = attn_schedule()
    NG = len(sched)

    VG = {}
    VR = {}
    VC = {}
    PREV = {}
    nv = 0
    last_par = {0: None, 1: None}
    pendp = None
    for u in range(4):
        for gi, (qg, kg, gt, ngq) in enumerate(sched):
            i = u * len(sched) + gi
            PREV[i] = last_par[gi % 2]
            last_par[gi % 2] = i
            nv += (1 if layer == 1 else 0) + sum(1 for _, mR in gt if mR is not None)
            VG[(u, gi)] = nv
            if kg == ngq - 1:
                if pendp is not None:
                    nv += 8
                    VR[pendp] = nv
                    nv += 4
                    VC[pendp] = nv
                pendp = (u, qg)
    nv += 8
    VR[pendp] = nv
    nv += 4
    VC[pendp] = nv

    with contextlib.ExitStack() as ctx:
        ent = ctx.enter_context
        qb = ent(nc.sbuf_tensor("qb", [128, 2 * S], BF16))
        kb = ent(nc.sbuf_tensor("kb", [128, 2 * S], BF16))
        vb = ent(nc.sbuf_tensor("vb", [128, 4 * 32 * 129], BF16))
        pt = ent(nc.sbuf_tensor("pt", [128, (28 + 32) * 512], BF16))
        PT_OFF = [0, 28 * 512]
        sq2 = ent(nc.sbuf_tensor("sq2", [128, 2 * 1536], BF16))
        msk = ent(nc.sbuf_tensor("msk", [128, 896], BF16))
        ids32 = ent(nc.sbuf_tensor("ids32", [128, 128], F32))
        rcp = ent(nc.sbuf_tensor("rcp", [128, 1], F32))
        cxf = ent(nc.sbuf_tensor("cxf", [128, 4 * 128], F32))
        cxT = ent(nc.sbuf_tensor("cxT", [128, 1024], BF16))
        psS1 = ent(nc.psum_tensor("psS1", [128, 1536], F32))
        psS2 = ent(nc.psum_tensor("psS2", [128, 1536], F32))
        psC = ent(nc.psum_tensor("psC", [128, 1024], F32))
        dC = sement(nc.semaphore("dC"))
        dU0 = sement(nc.semaphore("dU0"))
        dU1 = sement(nc.semaphore("dU1"))
        dV0 = sement(nc.semaphore("dV0"))
        dV1 = sement(nc.semaphore("dV1"))
        dS = sement(nc.semaphore("dS"))
        sS = sement(nc.semaphore("sS"))
        sA = sement(nc.semaphore("sA"))
        sV = sement(nc.semaphore("sV"))
        sT = sement(nc.semaphore("sT"))
        vv = sement(nc.semaphore("vv"))

        def qgi(u, qg):
            return u * 8 + qg

        with nc.Block() as blk:
            @blk.sync
            def _(e):
                e.dma_start(out=msk[:, :], in_=mk_d[:, :]).then_inc(dC, 16)
                e.dma_start(out=ids32[:, :], in_=id_d[:, :]).then_inc(dC, 16)

            @blk.vector
            def _(e):
                e.memset(vb[:, :], 1.0)

        DPU = 2
        qa_d, ka_d, vg_d = io["qa_d"], io["ka_d"], io["vg_d"]
        qav = qa_d[:, :].rearrange("(i hl p) t -> p i hl t", hl=2, p=128)
        kav = ka_d[:, :].rearrange("(i hl p) t -> p i hl t", hl=2, p=128)
        vbv = vb[:, :].rearrange("p (hl bb c d) -> p hl bb c d", hl=2, bb=2, d=129)
        with nc.Block() as blk:
            @blk.gpsimd
            def _(e):
                if coll is not None:
                    e.wait_ge(coll["cc"], coll["base"] + 2)
                for u in range(4):
                    if u >= 2:
                        e.wait_ge(dS, 64 * (u - 1))
                    o = (u % 2)
                    b, hl = u % 2, u // 2
                    dU = dU0 if u % 2 == 0 else dU1
                    e.dma_start(out=qb[:, o * S:(o + 1) * S].rearrange("p (j t) -> p j t", t=1024),
                                in_=qav[:, 4 * b:4 * b + 4, hl, :]).then_inc(dU, 16)
                    e.dma_start(out=kb[:, o * S:(o + 1) * S].rearrange("p (j t) -> p j t", t=1024),
                                in_=kav[:, 4 * b:4 * b + 4, hl, :]).then_inc(dU, 16)
                    if u == 1:
                        if coll is not None:
                            e.wait_ge(coll["cc"], coll["base"] + 3)
                        for b2 in range(2):
                            for hl2 in range(2):
                                e.dma_start(out=vbv[:, hl2, b2, :, 0:128],
                                            in_=vg_d[b2 * S:(b2 + 1) * S, :]
                                            .rearrange("(c p) (hl d) -> p hl c d", p=128, hl=2)[:, hl2]
                                            ).then_inc(dV0 if b2 == 0 else dV1, 16)
                if coll is not None:
                    e.wait_ge(dS, 256)
                    with allow_ici_all_to_all():
                        e.collective_compute("AllToAll", ALU.bypass,
                                             replica_groups=[list(range(8))],
                                             ins=[coll["ctx_in"].ap().opt()],
                                             outs=[coll["ctx_out"].ap().opt()]).then_inc(coll["cc"], 1)

            @blk.tensor
            def _(e):
                def emit_pv(u, qg, gi_last):
                    o = u
                    qq = qgi(u, qg)
                    if qq > 0:
                        pq = (u, qg - 1) if qg > 0 else (u - 1, 7)
                        e.wait_ge(vv, VC[pq])
                    e.wait_ge(sA, u * NG + gi_last + 1)
                    e.wait_ge(vv, VG[(u, gi_last)])
                    nkv = 4 * qg + 4
                    po = PT_OFF[qg % 2]
                    for qs in range(4):
                        for kvt in range(nkv):
                            mm = e.matmul(psC[:, qs * 256: qs * 256 + 129],
                                          pt[:, po + kvt * 512 + qs * 128:
                                             po + kvt * 512 + qs * 128 + 128],
                                          vb[:, o * 4128 + kvt * 129: o * 4128 + kvt * 129 + 129],
                                          start=(kvt == 0), stop=(kvt == nkv - 1))
                            if qs == 3 and kvt == nkv - 1:
                                mm.then_inc(sV, 1)
                    e.wait_ge(vv, VR[(u, qg)])
                    for qs in range(4):
                        e.transpose(psC[:, qs * 256: qs * 256 + 128],
                                    cxf[:, qs * 128:(qs + 1) * 128], ids32[:, :]).then_inc(sT, 1)

                pend = None
                for u in range(4):
                    if u == 0:
                        e.wait_ge(dC, 32)
                    e.wait_ge(dU0 if u % 2 == 0 else dU1, 16 * DPU * (u // 2 + 1))
                    e.wait_ge(dV0 if u % 2 == 0 else dV1, 32)
                    o = (u % 2)
                    for gi, (qg, kg, gt, ngq) in enumerate(sched):
                        i = u * NG + gi
                        psS = psS1 if gi % 2 == 0 else psS2
                        if PREV[i] is not None:
                            e.wait_ge(sA, PREV[i] + 1)
                        for j, (kvt, mR) in enumerate(gt):
                            mm = e.matmul(psS[:, j * 512:(j + 1) * 512],
                                          kb[:, o * S + kvt * 128: o * S + kvt * 128 + 128],
                                          qb[:, o * S + qg * 512: o * S + qg * 512 + 512],
                                          start=True, stop=True)
                            if j == len(gt) - 1:
                                mm.then_inc(sS, 1)
                        if kg == ngq - 1:
                            if pend is not None:
                                emit_pv(*pend)
                            pend = (u, qg, gi)
                if pend is not None:
                    emit_pv(*pend)

            @blk.scalar
            def _(e):
                for u in range(4):
                    for gi, (qg, kg, gt, ngq) in enumerate(sched):
                        i = u * NG + gi
                        psS = psS1 if gi % 2 == 0 else psS2
                        e.wait_ge(sS, i + 1)
                        w = 512 * len(gt)
                        po = PT_OFF[qg % 2] + gt[0][0] * 512
                        if qg >= 2:
                            e.wait_ge(sV, qgi(u, qg) - 1)
                        elif u >= 1 and qg < 2:
                            e.wait_ge(sV, qgi(u, qg) - 1)
                        if layer == 0:
                            e.activation(pt[:, po:po + w], psS[:, 0:w], AF.Exp,
                                         scale=ISQ).then_inc(sA, 1)
                        else:
                            if PREV[i] is not None:
                                pi = PREV[i]
                                e.wait_ge(vv, VG[(pi // NG, pi % NG)])
                            e.activation(sq2[:, (gi % 2) * 1536:(gi % 2) * 1536 + w],
                                         psS[:, 0:w], AF.Square).then_inc(sA, 1)

            @blk.vector
            def _(e):
                nv2 = 0

                def vop(inst):
                    nonlocal nv2
                    inst.then_inc(vv, 1)
                    nv2 += 1
                    return inst

                def emit_rc(u_, qg_):
                    nonlocal nv2
                    e.wait_ge(sV, qgi(u_, qg_) + 1)
                    if qg_ >= 2:
                        e.wait_ge(dS, 64 * u_ + 16 * (qg_ // 2))
                    elif u_ >= 1:
                        e.wait_ge(dS, 64 * u_)
                    for qs in range(4):
                        e.wait_ge(vv, nv2)
                        vop(e.reciprocal(rcp[:, :], psC[:, qs * 256 + 128: qs * 256 + 129]))
                        e.wait_ge(vv, nv2)
                        vop(e.tensor_scalar(cxf[:, qs * 128:(qs + 1) * 128],
                                            psC[:, qs * 256: qs * 256 + 128],
                                            rcp[:, :], None, ALU.mult, ALU.bypass))
                    assert nv2 == VR[(u_, qg_)]
                    e.wait_ge(sT, 4 * (qgi(u_, qg_) + 1))
                    for qs in range(4):
                        e.wait_ge(vv, nv2)
                        vop(e.tensor_copy(cxT[:, (qg_ % 2) * 512 + qs * 128: (qg_ % 2) * 512 + qs * 128 + 128],
                                          psC[:, qs * 256: qs * 256 + 128]))
                    assert nv2 == VC[(u_, qg_)]

                pendD = None
                for u in range(4):
                    for gi, (qg, kg, gt, ngq) in enumerate(sched):
                        i = u * NG + gi
                        e.wait_ge(sA, i + 1)
                        po = PT_OFF[qg % 2] + gt[0][0] * 512
                        w = 512 * len(gt)
                        if kg == 0 and qgi(u, qg) >= 2:
                            e.wait_ge(sV, qgi(u, qg) - 1)
                        if layer == 1:
                            e.wait_ge(vv, nv2)
                            vop(e.tensor_tensor(pt[:, po:po + w],
                                                sq2[:, (gi % 2) * 1536:(gi % 2) * 1536 + w],
                                                sq2[:, (gi % 2) * 1536:(gi % 2) * 1536 + w], ALU.mult))
                        for j, (kvt, mR) in enumerate(gt):
                            if mR is not None:
                                e.wait_ge(vv, nv2)
                                vop(e.tensor_tensor(pt[:, PT_OFF[qg % 2] + kvt * 512:
                                                       PT_OFF[qg % 2] + kvt * 512 + 512],
                                                    pt[:, PT_OFF[qg % 2] + kvt * 512:
                                                       PT_OFF[qg % 2] + kvt * 512 + 512],
                                                    msk[:, 384 - mR: 384 - mR + 512], ALU.mult))
                        assert nv2 == VG[(u, gi)], (u, gi, nv2, VG[(u, gi)])
                        if kg == ngq - 1:
                            if pendD is not None:
                                emit_rc(*pendD)
                            pendD = (u, qg)
                if pendD is not None:
                    emit_rc(*pendD)

            @blk.sync
            def _(e):
                ctxav = ctxa_d[:, :].rearrange("(i hl p) t -> p i hl t", hl=2, p=128)
                for u in range(4):
                    b, hl = u % 2, u // 2
                    for jj in range(4):
                        e.wait_ge(vv, VC[(u, 2 * jj + 1)])
                        e.dma_start(out=ctxav[:, 4 * b + jj, hl, :],
                                    in_=cxT[:, 0:1024]).then_inc(dS, 16)
                e.wait_ge(dS, 256)


def emit_moe(nc, ctx, cnt, env):
    ent = env["ent"]
    xsb, junk, psA = env["xsb"], env["junk"], env["psA"]
    tMM, aa, vv, dX = env["tMM"], env["aa"], env["vv"], env["dX"]
    g_load, g_wait, gld = env["g_load"], env["g_wait"], env["gld"]
    wg_d, wgu_d, wdn_d = env["wg_d"], env["wgu_d"], env["wdn_d"]
    wsgu_d, wsdn_d, sel_d = env["wsgu_d"], env["wsdn_d"], env["sel_d"]
    ids32 = env["ids32"]

    xT = ent(nc.sbuf_tensor("xTm", [128, 2 * 16 * 512], BF16))
    wg8 = ent(nc.sbuf_tensor("wg8", [128, 16 * 8], BF16))
    sels = ent(nc.sbuf_tensor("sels", [8, 1024], BF16))
    gsl = ent(nc.sbuf_tensor("gsl", [128, 3 * 4096], BF16))
    dsl = ent(nc.sbuf_tensor("dsl", [128, 3 * 4096], BF16))
    actb = ent(nc.sbuf_tensor("actb", [128, 2 * 16 * 512], BF16))
    stmp = ent(nc.sbuf_tensor("stmp", [128, 2 * 512], BF16))
    lgt = ent(nc.sbuf_tensor("lgt", [128, 8], F32))
    prb = ent(nc.sbuf_tensor("prb", [128, 8], F32))
    prc = ent(nc.sbuf_tensor("prc", [128, 8], F32))
    kpb = ent(nc.sbuf_tensor("kpb", [128, 8], F32))
    cmbf = ent(nc.sbuf_tensor("cmbf", [128, 4 * 8], F32))
    c8g = ent(nc.sbuf_tensor("c8g", [8, 2 * 512], BF16))
    bce = ent(nc.sbuf_tensor("bce", [128, 2 * 8 * 512], BF16))
    m1b = ent(nc.sbuf_tensor("m1b", [128, 1], F32))
    m2b = ent(nc.sbuf_tensor("m2b", [128, 1], F32))
    nmx = ent(nc.sbuf_tensor("nmx", [128, 1], F32))
    rs1 = ent(nc.sbuf_tensor("rs1", [128, 1], F32))
    rc1 = ent(nc.sbuf_tensor("rc1", [128, 1], F32))
    dn2 = ent(nc.sbuf_tensor("dn2", [128, 1], F32))
    rc2 = ent(nc.sbuf_tensor("rc2", [128, 1], F32))
    psEO = [ent(nc.psum_tensor("psEO%d" % i, [128, 512], F32)) for i in range(4)]

    wg8_li = gld["i"]
    with nc.Block() as blk:
        @blk.gpsimd
        def _(e):
            g_load(e, wg8[:, :].rearrange("p (kc n) -> p kc n", n=8),
                   wg_d[:, :].rearrange("(kc p) n -> p kc n", p=128))

        @blk.sync
        def _(e):
            e.dma_start(out=sels[:, :], in_=sel_d[:, :]).then_inc(dX, 16)
    cnt["dx"] += 1
    NDX = cnt["dx"]

    T0, V0 = cnt["t"], cnt["v"]
    with nc.Block() as blk:
        @blk.tensor
        def _(e):
            for i in range(128):
                g, r = divmod(i, 64)
                tt, dc = divmod(r, 16)
                if i >= 2:
                    e.wait_ge(vv, V0 + i - 1)
                e.transpose(psEO[i % 2][:, 0:128],
                            xsb[:, (g * 4 + tt) * D + dc * 128: (g * 4 + tt) * D + dc * 128 + 128],
                            ids32[:, :]).then_inc(tMM, 1)

        @blk.vector
        def _(e):
            for i in range(128):
                g, r = divmod(i, 64)
                tt, dc = divmod(r, 16)
                e.wait_ge(tMM, T0 + i + 1)
                e.wait_ge(vv, V0 + i)
                e.tensor_copy(xT[:, g * 8192 + dc * 512 + tt * 128: g * 8192 + dc * 512 + tt * 128 + 128],
                              psEO[i % 2][:, 0:128]).then_inc(vv, 1)
    cnt["t"] += 128
    cnt["v"] += 128

    segs = []
    base = {"t": cnt["t"], "v": cnt["v"], "a": cnt["a"]}
    lcur = gld["i"]

    def addseg(kind, **kw):
        nonlocal lcur
        kw["T0"], kw["V0"], kw["A0"], kw["li0"] = base["t"], base["v"], base["a"], lcur
        segs.append((kind, kw))
        if kind == "router":
            base["t"] += 16; base["v"] += 72; base["a"] += 4
        elif kind == "gu":
            vh = 2 if kw["scale_of"] is None else 4
            base["t"] += 32; base["v"] += 16 * vh; base["a"] += 32
            lcur += 16
        else:
            base["t"] += 32; base["v"] += 32
            lcur += 8

    addseg("router", g=0)
    addseg("router", g=1)
    for su in range(2):
        addseg("gu", first=(su == 0),
               srcs_of=lambda s, su=su: (
                   wsgu_d[:, su * 2048 + s * 256: su * 2048 + (s + 1) * 256].rearrange("(kc p) n -> p kc n", p=128),
                   wsgu_d[:, 4096 + su * 2048 + s * 256: 4096 + su * 2048 + (s + 1) * 256].rearrange("(kc p) n -> p kc n", p=128)),
               scale_of=None, chunk_base=0)
        addseg("eo", first=(su == 0),
               slab_src_of=lambda oc, hf, su=su: wsdn_d[su * 2048 + hf * 1024: su * 2048 + (hf + 1) * 1024,
                                                        oc * 512:(oc + 1) * 512]
               .rearrange("(kc p) n -> p kc n", p=128), out_base=0)
    for d in range(4):
        addseg("gu", first=False,
               srcs_of=lambda s, d=d: (
                   wgu_d[2 * d + s // 4, :, (s % 4) * 256:((s % 4) + 1) * 256].rearrange("(kc p_) n -> p_ kc n", p_=128),
                   wgu_d[2 * d + s // 4, :, 1024 + (s % 4) * 256: 1024 + ((s % 4) + 1) * 256].rearrange("(kc p_) n -> p_ kc n", p_=128)),
               scale_of=lambda s, d=d: 2 * d + s // 4, chunk_base=0)
        addseg("eo", first=False,
               slab_src_of=lambda oc, hf, d=d: wdn_d[2 * d + hf, :, oc * 512:(oc + 1) * 512]
               .rearrange("(ic p_) n -> p_ ic n", p_=128), out_base=0)

    with nc.Block() as blk:
        @blk.gpsimd
        def _(e):
            for kind, kw in segs:
                T0 = kw["T0"]
                if kind == "gu":
                    for L in range(16):
                        s, j = divmod(L, 2)
                        if L >= 3:
                            e.wait_ge(tMM, T0 + 4 * ((L - 3) // 2 + 1))
                        elif not kw["first"]:
                            e.wait_ge(tMM, T0 - 36 if L == 1 else T0 - 32)
                        srcg, srcu = kw["srcs_of"](s)
                        g_load(e, gsl[:, (L % 3) * 4096:(L % 3 + 1) * 4096].rearrange("p (kc n) -> p kc n", n=256),
                               srcg if j == 0 else srcu)
                elif kind == "eo":
                    for H in range(8):
                        oc, hf = divmod(H, 2)
                        if H >= 3:
                            e.wait_ge(tMM, T0 + 8 * ((H - 3) // 2 + 1))
                        elif not kw["first"]:
                            e.wait_ge(tMM, T0 - 40 if H == 2 else T0 - 32)
                        g_load(e, dsl[:, (H % 3) * 4096:(H % 3 + 1) * 4096].rearrange("p (kc n) -> p kc n", n=512),
                               kw["slab_src_of"](oc, hf))

        @blk.tensor
        def _(e):
            for kind, kw in segs:
                T0, V0, li0 = kw["T0"], kw["V0"], kw["li0"]
                if kind == "router":
                    g = kw["g"]
                    for tt in range(4):
                        if g == 0 and tt == 0:
                            e.wait_ge(dX, 16 * NDX)
                            g_wait(e, wg8_li)
                        if g == 1:
                            e.wait_ge(vv, V0 - 3 + tt)
                        for hc in range(16):
                            mm = e.matmul(psA[:, tt * 512: tt * 512 + 8],
                                          xT[:, g * 8192 + hc * 512 + tt * 128: g * 8192 + hc * 512 + tt * 128 + 128],
                                          wg8[:, hc * 8: hc * 8 + 8],
                                          start=(hc == 0), stop=(hc == 15))
                            if hc == 15:
                                mm.then_inc(tMM, 1)
                    for tt in range(4):
                        e.wait_ge(vv, V0 + 60 + tt)
                        e.transpose(psEO[2][0:8, 0:128], cmbf[:, tt * 8:(tt + 1) * 8],
                                    ids32[:, :]).then_inc(tMM, 1)
                    for ex in range(8):
                        e.wait_ge(vv, V0 + 64)
                        if ex >= 4:
                            e.wait_ge(vv, V0 + 64 + ex - 3)
                        e.matmul(psA[:, (ex % 4) * 512:(ex % 4) * 512 + 512],
                                 sels[:, ex * 128:(ex + 1) * 128],
                                 c8g[:, g * 512:(g + 1) * 512], start=True, stop=True).then_inc(tMM, 1)
                elif kind == "gu":
                    VH = 2 if kw["scale_of"] is None else 4
                    for s in range(8):
                        g_wait(e, li0 + 2 * s)
                        g_wait(e, li0 + 2 * s + 1)
                        for tg in range(2):
                            h = 2 * s + tg
                            if h >= 1:
                                e.wait_ge(vv, V0 + VH * h)
                            elif kw["first"]:
                                e.wait_ge(vv, V0)
                            else:
                                e.wait_ge(vv, V0 - 32)
                            for ci in range(4):
                                sl = (2 * s + ci // 2) % 3
                                for hc in range(16):
                                    mm = e.matmul(psA[:, ci * 512:(ci + 1) * 512],
                                                  gsl[:, sl * 4096 + hc * 256 + (ci % 2) * 128:
                                                      sl * 4096 + hc * 256 + (ci % 2) * 128 + 128],
                                                  xT[:, tg * 8192 + hc * 512: tg * 8192 + hc * 512 + 512],
                                                  start=(hc == 0), stop=(hc == 15))
                                    if ci % 2 == 1 and hc == 15:
                                        mm.then_inc(tMM, 1)
                else:
                    for oc in range(4):
                        g_wait(e, li0 + 2 * oc)
                        g_wait(e, li0 + 2 * oc + 1)
                        for g in range(2):
                            for tt in range(4):
                                if g == 1:
                                    e.wait_ge(vv, V0 + 8 * oc + tt + 1)
                                elif oc >= 1:
                                    e.wait_ge(vv, V0 + 8 * (oc - 1) + 4 + tt + 1)
                                else:
                                    e.wait_ge(vv, V0)
                                for ic in range(16):
                                    dcol = ((2 * oc + ic // 8) % 3) * 4096 + (ic % 8) * 512
                                    mm = e.matmul(psEO[tt][:, 0:512],
                                                  actb[:, (g * 16 + ic) * 512 + tt * 128: (g * 16 + ic) * 512 + tt * 128 + 128],
                                                  dsl[:, dcol: dcol + 512],
                                                  start=(ic == 0), stop=(ic == 15))
                                    if ic == 15:
                                        mm.then_inc(tMM, 1)

        @blk.scalar
        def _(e):
            for kind, kw in segs:
                T0, V0, A0 = kw["T0"], kw["V0"], kw["A0"]
                if kind == "router":
                    for tt in range(4):
                        e.wait_ge(aa, A0 + tt)
                        e.wait_ge(vv, V0 + 15 * tt + 3)
                        e.activation(prb[:, :], lgt[:, :], AF.Exp, bias=nmx[:, :]).then_inc(aa, 1)
                elif kind == "gu":
                    VH = 2 if kw["scale_of"] is None else 4
                    for h in range(16):
                        for j in range(2):
                            e.wait_ge(aa, A0 + 2 * h + j)
                            e.wait_ge(tMM, T0 + 2 * h + 1)
                            if h >= 1:
                                e.wait_ge(vv, V0 + VH * (h - 1) + (VH // 2) * j + 1)
                            elif not kw["first"]:
                                e.wait_ge(vv, V0 - 32)
                            e.activation(stmp[:, j * 512:(j + 1) * 512],
                                         psA[:, j * 512:(j + 1) * 512], AF.Silu).then_inc(aa, 1)

        @blk.vector
        def _(e):
            for kind, kw in segs:
                T0, V0, A0 = kw["T0"], kw["V0"], kw["A0"]
                nv = V0
                if kind == "router":
                    g = kw["g"]

                    def vop(inst):
                        nonlocal nv
                        inst.then_inc(vv, 1); nv += 1

                    for tt in range(4):
                        e.wait_ge(tMM, T0 + tt + 1)
                        e.wait_ge(vv, nv); vop(e.tensor_copy(lgt[:, :], psA[:, tt * 512: tt * 512 + 8]))
                        e.wait_ge(vv, nv); vop(e.reduce_max(m1b[:, :], lgt[:, :], axis=AX.X))
                        e.wait_ge(vv, nv); vop(e.tensor_scalar(nmx[:, :], m1b[:, :], -1.0, None, ALU.mult, ALU.bypass))
                        e.wait_ge(aa, A0 + tt + 1)
                        e.wait_ge(vv, nv); vop(e.reduce_sum(rs1[:, :], prb[:, :], axis=AX.X))
                        e.wait_ge(vv, nv); vop(e.reciprocal(rc1[:, :], rs1[:, :]))
                        e.wait_ge(vv, nv); vop(e.tensor_scalar(prc[:, :], prb[:, :], rc1[:, :], None, ALU.mult, ALU.bypass))
                        e.wait_ge(vv, nv); vop(e.reduce_max(m1b[:, :], prc[:, :], axis=AX.X))
                        e.wait_ge(vv, nv); vop(e.tensor_scalar(kpb[:, :], prc[:, :], m1b[:, :], None, ALU.is_ge, ALU.bypass))
                        e.wait_ge(vv, nv); vop(e.scalar_tensor_tensor(prb[:, :], kpb[:, :], -1e9, prc[:, :], ALU.mult, ALU.add))
                        e.wait_ge(vv, nv); vop(e.reduce_max(m2b[:, :], prb[:, :], axis=AX.X))
                        e.wait_ge(vv, nv); vop(e.tensor_scalar(kpb[:, :], prc[:, :], m2b[:, :], None, ALU.is_ge, ALU.bypass))
                        e.wait_ge(vv, nv); vop(e.tensor_tensor(dn2[:, :], m1b[:, :], m2b[:, :], ALU.add))
                        e.wait_ge(vv, nv); vop(e.reciprocal(rc2[:, :], dn2[:, :]))
                        e.wait_ge(vv, nv); vop(e.tensor_tensor(cmbf[:, tt * 8:(tt + 1) * 8], prc[:, :], kpb[:, :], ALU.mult))
                        e.wait_ge(vv, nv); vop(e.tensor_scalar(cmbf[:, tt * 8:(tt + 1) * 8], cmbf[:, tt * 8:(tt + 1) * 8], rc2[:, :], None, ALU.mult, ALU.bypass))
                    assert nv == V0 + 60, nv
                    for tt in range(4):
                        e.wait_ge(tMM, T0 + 4 + tt + 1)
                        e.wait_ge(vv, nv)
                        vop(e.tensor_copy(c8g[:, g * 512 + tt * 128: g * 512 + (tt + 1) * 128], psEO[2][0:8, 0:128]))
                    for ex in range(8):
                        e.wait_ge(tMM, T0 + 8 + ex + 1)
                        e.wait_ge(vv, nv)
                        vop(e.tensor_copy(bce[:, (g * 8 + ex) * 512:(g * 8 + ex + 1) * 512],
                                          psA[:, (ex % 4) * 512:(ex % 4) * 512 + 512]))
                    assert nv == V0 + 72
                elif kind == "gu":
                    VH = 2 if kw["scale_of"] is None else 4
                    for s in range(8):
                        for tg in range(2):
                            h = 2 * s + tg
                            for j in range(2):
                                e.wait_ge(aa, A0 + 2 * h + j + 1)
                                e.wait_ge(tMM, T0 + 2 * h + 2)
                                ao = (tg * 16 + kw["chunk_base"] + 2 * s + j) * 512
                                e.wait_ge(vv, nv)
                                e.tensor_tensor(actb[:, ao: ao + 512],
                                                stmp[:, j * 512:(j + 1) * 512],
                                                psA[:, (2 + j) * 512:(3 + j) * 512], ALU.mult).then_inc(vv, 1)
                                nv += 1
                                if kw["scale_of"] is not None:
                                    so = (tg * 8 + kw["scale_of"](s)) * 512
                                    e.wait_ge(vv, nv)
                                    e.tensor_tensor(actb[:, ao: ao + 512], actb[:, ao: ao + 512],
                                                    bce[:, so: so + 512], ALU.mult).then_inc(vv, 1)
                                    nv += 1
                else:
                    for oc in range(4):
                        for g in range(2):
                            for tt in range(4):
                                e.wait_ge(tMM, T0 + 8 * oc + 4 * g + tt + 1)
                                xsl = xsb[:, (g * 4 + tt) * D + kw["out_base"] + oc * 512:
                                          (g * 4 + tt) * D + kw["out_base"] + oc * 512 + 512]
                                e.wait_ge(vv, nv)
                                e.tensor_tensor(xsl, psEO[tt][:, 0:512], xsl, ALU.add).then_inc(vv, 1)
                                nv += 1

    cnt["t"], cnt["v"], cnt["a"] = base["t"], base["v"], base["a"]
    assert gld["i"] == lcur


def emit_bitnet(nc, ctx, cnt, env):
    ent = env["ent"]
    xsb, junk, psA = env["xsb"], env["junk"], env["psA"]
    tMM, aa, vv, dX = env["tMM"], env["aa"], env["vv"], env["dX"]
    g_load, g_wait, gld = env["g_load"], env["g_wait"], env["gld"]
    wq_d, o1f_d, orf_d, idb_d = env["wq_d"], env["o1f_d"], env["orf_d"], env["idb_d"]
    eps = env["eps"]

    wfs = ent(nc.sbuf_tensor("wfs", [128, 4 * 2048], F32))
    wqb = ent(nc.sbuf_tensor("wqb", [128, 4 * 2048], BF16))
    wqT = ent(nc.sbuf_tensor("wqT", [128, 16 * 512], BF16))
    xqs = ent(nc.sbuf_tensor("xqs", [128, 4 * 2048], BF16))
    xqT = ent(nc.sbuf_tensor("xqT", [128, 2 * 16 * 512], BF16))
    o1f = ent(nc.sbuf_tensor("o1f", [128, 1], F32))
    orf = ent(nc.sbuf_tensor("orf", [1, 128], F32))
    idb = ent(nc.sbuf_tensor("idb", [128, 128], BF16))
    tabs = ent(nc.sbuf_tensor("tabs", [128, 1], F32))
    asum = ent(nc.sbuf_tensor("asum", [128, 2], F32))
    af = ent(nc.sbuf_tensor("af", [1, 2], F32))
    ab = ent(nc.sbuf_tensor("ab", [128, 1], F32))
    rba = ent(nc.sbuf_tensor("rba", [128, 1], F32))
    rmaxb = ent(nc.sbuf_tensor("rmaxb", [128, 1], F32))
    betab = ent(nc.sbuf_tensor("betab", [128, 1], F32))
    rbb = ent(nc.sbuf_tensor("rbb", [128, 1], F32))
    bab = ent(nc.sbuf_tensor("bab", [128, 8], F32))
    psQ1 = ent(nc.psum_tensor("psQ1", [128, 128], BF16))
    psQ2 = ent(nc.psum_tensor("psQ2", [128, 128], BF16))
    psSm = ent(nc.psum_tensor("psSm", [128, 512], F32))

    T0, V0, A0 = cnt["t"], cnt["v"], cnt["a"]
    li0 = gld["i"]
    with nc.Block() as blk:
        @blk.sync
        def _(e):
            e.dma_start(out=o1f[:, :], in_=o1f_d[:, :]).then_inc(dX, 16)
            e.dma_start(out=orf[:, :], in_=orf_d[:, :]).then_inc(dX, 16)
            e.dma_start(out=idb[:, :], in_=idb_d[:, :]).then_inc(dX, 16)
        cnt["dx"] += 3

        @blk.gpsimd
        def _(e):
            for st in range(16):
                if st >= 2:
                    e.wait_ge(aa, A0 + st - 1)
                g_load(e, wfs[:, (st % 2) * 2048:(st % 2) * 2048 + 2048],
                       wq_d[st * 128:(st + 1) * 128, :])

        @blk.scalar
        def _(e):
            for st in range(16):
                e.wait_ge(aa, A0 + st)
                g_wait(e, li0 + st)
                if st >= 1:
                    e.wait_ge(vv, V0 + 1 + st)
                e.activation(junk[:, :], wfs[:, (st % 2) * 2048:(st % 2) * 2048 + 2048],
                             AF.Abs, accum_out=tabs[:, :]).then_inc(aa, 1)
            cnt["a"] = A0 + 16

        @blk.vector
        def _(e):
            nv = V0
            e.wait_ge(vv, nv)
            e.memset(asum[:, :], 0.0).then_inc(vv, 1); nv += 1
            for st in range(16):
                e.wait_ge(aa, A0 + st + 1)
                e.wait_ge(vv, nv)
                e.tensor_tensor(asum[:, 0:1], tabs[:, :], asum[:, 0:1], ALU.add).then_inc(vv, 1)
                nv += 1
            cnt["v"] = nv
    cnt["t"] = T0

    T0, V0, A0 = cnt["t"], cnt["v"], cnt["a"]
    NDX = cnt["dx"]
    with nc.Block() as blk:
        @blk.tensor
        def _(e):
            e.wait_ge(dX, 16 * NDX)
            e.matmul(psSm[0:1, 0:2], o1f[:, :], asum[:, 0:2], start=True, stop=True).then_inc(tMM, 1)
            e.wait_ge(vv, V0 + 2)
            e.matmul(psSm[:, 0:2], orf[:, :], af[:, :], start=True, stop=True).then_inc(tMM, 1)

        @blk.vector
        def _(e):
            nv = V0
            e.wait_ge(tMM, T0 + 1)
            e.wait_ge(vv, nv)
            e.tensor_scalar(af[:, :], psSm[0:1, 0:2], 1.0 / 4194304.0, 1e-8,
                            ALU.mult, ALU.max).then_inc(vv, 1); nv += 1
            e.wait_ge(vv, nv)
            e.tensor_copy(af[:, :], af[:, :]).then_inc(vv, 1); nv += 1
            e.wait_ge(tMM, T0 + 2)
            e.wait_ge(vv, nv)
            e.tensor_copy(ab[:, :], psSm[:, 0:1]).then_inc(vv, 1); nv += 1
            e.wait_ge(vv, nv)
            e.reciprocal(rba[:, :], ab[:, :]).then_inc(vv, 1); nv += 1
            cnt["v"] = nv
    cnt["t"] = T0 + 2

    T0, V0, A0 = cnt["t"], cnt["v"], cnt["a"]
    with nc.Block() as blk:
        @blk.tensor
        def _(e):
            for i in range(128):
                t8, ic = divmod(i, 16)
                if i >= 2:
                    e.wait_ge(aa, A0 + i - 1)
                e.wait_ge(vv, V0 + 6 * t8 + 5)
                e.transpose((psQ1 if i % 2 == 0 else psQ2)[:, :],
                            xqs[:, (t8 % 4) * 2048 + ic * 128: (t8 % 4) * 2048 + (ic + 1) * 128],
                            idb[:, :]).then_inc(tMM, 1)

        @blk.scalar
        def _(e):
            for i in range(128):
                t8, ic = divmod(i, 16)
                th, tl = divmod(t8, 4)
                e.wait_ge(tMM, T0 + i + 1)
                e.activation(xqT[:, th * 8192 + ic * 512 + tl * 128:
                                th * 8192 + ic * 512 + tl * 128 + 128],
                             (psQ1 if i % 2 == 0 else psQ2)[:, :],
                             AF.Identity).then_inc(aa, 1)
            cnt["a"] = A0 + 128

        @blk.vector
        def _(e):
            nv = V0
            for t8 in range(8):
                xt = xsb[:, t8 * D:(t8 + 1) * D]
                if t8 >= 4:
                    e.wait_ge(tMM, T0 + 16 * (t8 - 3))
                e.wait_ge(vv, nv)
                e.reduce_max(rmaxb[:, :], xt, axis=AX.X, apply_absolute_value=True).then_inc(vv, 1); nv += 1
                e.wait_ge(vv, nv)
                e.tensor_scalar(betab[:, :], rmaxb[:, :], 1.0 / 127.0, 1e-8,
                                ALU.mult, ALU.max).then_inc(vv, 1); nv += 1
                e.wait_ge(vv, nv)
                e.reciprocal(rbb[:, :], betab[:, :]).then_inc(vv, 1); nv += 1
                e.wait_ge(vv, nv)
                e.tensor_scalar(junk[:, :], xt, rbb[:, :], RC, ALU.mult, ALU.add).then_inc(vv, 1); nv += 1
                e.wait_ge(vv, nv)
                e.tensor_scalar(xqs[:, (t8 % 4) * 2048:((t8 % 4) + 1) * 2048], junk[:, :], RC, None,
                                ALU.subtract, ALU.bypass).then_inc(vv, 1); nv += 1
                e.wait_ge(vv, nv)
                e.tensor_tensor(bab[:, t8: t8 + 1], ab[:, :], betab[:, :], ALU.mult).then_inc(vv, 1); nv += 1
            cnt["v"] = nv
    cnt["t"] = T0 + 128

    for ocg in range(4):
        T0, V0, A0 = cnt["t"], cnt["v"], cnt["a"]
        li0 = gld["i"]
        with nc.Block() as blk:
            @blk.gpsimd
            def _(e):
                g_load(e, wfs[:, :].rearrange("p (oc i) -> p oc i", i=D),
                       wq_d[ocg * 512:(ocg + 1) * 512, :].rearrange("(oc p) i -> p oc i", p=128))

            @blk.vector
            def _(e):
                nv = V0
                g_wait(e, li0)
                for oc in range(4):
                    wf = wfs[:, oc * 2048:(oc + 1) * 2048]
                    wq = wqb[:, oc * 2048:(oc + 1) * 2048]
                    e.wait_ge(vv, nv)
                    e.tensor_scalar(junk[:, :], wf, rba[:, :], RC, ALU.mult, ALU.add).then_inc(vv, 1); nv += 1
                    e.wait_ge(vv, nv)
                    e.tensor_scalar(junk[:, :], junk[:, :], RC, -1.0, ALU.subtract, ALU.max).then_inc(vv, 1); nv += 1
                    e.wait_ge(vv, nv)
                    e.tensor_scalar(wq, junk[:, :], 1.0, None, ALU.min, ALU.bypass).then_inc(vv, 1); nv += 1
                for tl in range(8):
                    e.wait_ge(tMM, T0 + 64 + tl + 1)
                    xsl = xsb[:, tl * D + ocg * 512: tl * D + ocg * 512 + 512]
                    e.wait_ge(vv, nv)
                    e.scalar_tensor_tensor(xsl, psA[:, (tl % 4) * 512:((tl % 4) + 1) * 512],
                                           bab[:, tl: tl + 1], xsl,
                                           ALU.mult, ALU.add).then_inc(vv, 1)
                    nv += 1
                cnt["v"] = nv

            @blk.scalar
            def _(e):
                for i in range(64):
                    oc, ic = divmod(i, 16)
                    e.wait_ge(tMM, T0 + i + 1)
                    e.activation(wqT[:, ic * 512 + oc * 128: ic * 512 + oc * 128 + 128],
                                 (psQ1 if i % 2 == 0 else psQ2)[:, :],
                                 AF.Identity).then_inc(aa, 1)
                cnt["a"] = A0 + 64

            @blk.tensor
            def _(e):
                for i in range(64):
                    oc, ic = divmod(i, 16)
                    e.wait_ge(vv, V0 + 3 * (oc + 1))
                    if i >= 2:
                        e.wait_ge(aa, A0 + i - 1)
                    e.transpose((psQ1 if i % 2 == 0 else psQ2)[:, :],
                                wqb[:, oc * 2048 + ic * 128: oc * 2048 + ic * 128 + 128],
                                idb[:, :]).then_inc(tMM, 1)
                for tl in range(8):
                    e.wait_ge(aa, A0 + 64)
                    if tl >= 4:
                        e.wait_ge(vv, V0 + 12 + (tl - 4) + 1)
                    for ic in range(16):
                        mm = e.matmul(psA[:, (tl % 4) * 512:((tl % 4) + 1) * 512],
                                      xqT[:, (tl // 4) * 8192 + ic * 512 + (tl % 4) * 128:
                                          (tl // 4) * 8192 + ic * 512 + (tl % 4) * 128 + 128],
                                      wqT[:, ic * 512: ic * 512 + 512],
                                      start=(ic == 0), stop=(ic == 15))
                        if ic == 15:
                            mm.then_inc(tMM, 1)
        cnt["t"] = T0 + 72


def build_post(layer):
    nc = bass.Bass("TRN2", target_bir_lowering=False)
    io = {}
    io["x_d"] = nc.dram_tensor("x", [TPC, D], F32, kind="ExternalInput")
    io["cxo_d"] = nc.dram_tensor("ctxo", [D, TPC], BF16, kind="ExternalInput")
    io["wo_d"] = nc.dram_tensor("wout", [D, D], F32, kind="ExternalInput")
    io["id_d"] = nc.dram_tensor("ident32", [128, 128], F32, kind="ExternalInput")
    if layer == 0:
        io["bo_d"] = nc.dram_tensor("boutbc", [128, D], F32, kind="ExternalInput")
        io["wg_d"] = nc.dram_tensor("wgate", [D, 8], F32, kind="ExternalInput")
        io["wgu_d"] = nc.dram_tensor("wgu", [8, D, D], F32, kind="ExternalInput")
        io["wdn_d"] = nc.dram_tensor("wdn", [8, 1024, D], F32, kind="ExternalInput")
        io["wsgu_d"] = nc.dram_tensor("wsgu", [D, 8192], F32, kind="ExternalInput")
        io["wsdn_d"] = nc.dram_tensor("wsdn", [4096, D], F32, kind="ExternalInput")
        io["sel_d"] = nc.dram_tensor("sel", [8, 1024], BF16, kind="ExternalInput")
    else:
        io["wq_d"] = nc.dram_tensor("wq", [D, D], F32, kind="ExternalInput")
        io["o1f_d"] = nc.dram_tensor("ones128f", [128, 1], F32, kind="ExternalInput")
        io["orf_d"] = nc.dram_tensor("onesrowf", [1, 128], F32, kind="ExternalInput")
        io["idb_d"] = nc.dram_tensor("identb", [128, 128], BF16, kind="ExternalInput")
    io["xo_d"] = nc.dram_tensor("xo", [TPC, D], F32, kind="ExternalOutput")
    with contextlib.ExitStack() as ctx:
        xsb = ctx.enter_context(nc.sbuf_tensor("xsb", [128, 8 * D], F32))
        emit_post(nc, layer, xsb, io, ctx.enter_context, load_x=True, store=True)
    return nc


def emit_post(nc, layer, xsb, io, sement, load_x, store, coll=None):
    x_d = io.get("x_d")
    wo_d, id_d = io["wo_d"], io["id_d"]
    if layer == 0:
        bo_d = io["bo_d"]
        wg_d, wgu_d, wdn_d = io["wg_d"], io["wgu_d"], io["wdn_d"]
        wsgu_d, wsdn_d, sel_d = io["wsgu_d"], io["wsdn_d"], io["sel_d"]
    else:
        wq_d = io["wq_d"]
        o1f_d, orf_d, idb_d = io["o1f_d"], io["orf_d"], io["idb_d"]
    xo_d = io.get("xo_d")

    EV = 2 if layer == 0 else 1

    with contextlib.ExitStack() as ctx:
        ent = ctx.enter_context

        junk = ent(nc.sbuf_tensor("junk", [128, 2048], F32))
        ids32 = ent(nc.sbuf_tensor("ids32", [128, 128], F32))
        eps = ent(nc.sbuf_tensor("eps", [128, 1], F32))
        srow = ent(nc.sbuf_tensor("srow", [128, 1], F32))
        sqs = ent(nc.sbuf_tensor("sqs", [128, 1], F32))
        mu = ent(nc.sbuf_tensor("mu", [128, 1], F32))
        msq = ent(nc.sbuf_tensor("msq", [128, 1], F32))
        e2b = ent(nc.sbuf_tensor("e2b", [128, 1], F32))
        varb = ent(nc.sbuf_tensor("varb", [128, 1], F32))
        lnvb = ent(nc.sbuf_tensor("lnvb", [128, 1], F32))
        rstd = ent(nc.sbuf_tensor("rstd", [128, 1], F32))
        mrs = ent(nc.sbuf_tensor("mrs", [128, 1], F32))
        dX = sement(nc.semaphore("dX"))
        dA0 = sement(nc.semaphore("dA0"))
        dA1 = sement(nc.semaphore("dA1"))
        dB = [sement(nc.semaphore("dB%d" % i)) for i in range(4)]
        dCX = sement(nc.semaphore("dCX"))
        dS = sement(nc.semaphore("dS"))
        tMM = sement(nc.semaphore("tMM"))
        aa = sement(nc.semaphore("aa"))
        vv = sement(nc.semaphore("vv"))
        cnt = {"t": 0, "a": 0, "v": 0}
        cnt["dx"] = 0
        opctx = contextlib.ExitStack()
        cxs = opctx.enter_context(nc.sbuf_tensor("cxs", [128, 16 * TPC], BF16))
        wsl = opctx.enter_context(nc.sbuf_tensor("wsl", [128, 2 * 16 * 512], BF16))
        scr = opctx.enter_context(nc.sbuf_tensor("scr", [128, 512], F32))
        if layer == 0:
            bo = opctx.enter_context(nc.sbuf_tensor("bo", [128, D], F32))
        gld = {"i": 0, "n": [0, 0, 0, 0]}

        def g_load(e, dst_ap, src_ap):
            p = gld["i"] % 4
            gld["i"] += 1
            if gld["n"][p] >= 1:
                e.wait_ge(dB[p], 16 * gld["n"][p])
            gld["n"][p] += 1
            e.dma_start(out=dst_ap, in_=src_ap).then_inc(dB[p], 16)

        def g_wait(e, li):
            e.wait_ge(dB[li % 4], 16 * (li // 4 + 1))

        with nc.Block() as blk:
            @blk.sync
            def _(e):
                if load_x:
                    e.dma_start(out=xsb[:, :].rearrange("p (tt d) -> p tt d", d=D),
                                in_=x_d[:, :].rearrange("(tt p) d -> p tt d", p=128)).then_inc(dX, 16)
                cxo_d = io["cxo_d"]
                if coll is not None:
                    e.wait_ge(coll["cc"], coll["base"] + 4)
                e.dma_start(out=cxs[:, :].rearrange("p (q r t) -> p q r t", r=2, t=TPC),
                            in_=cxo_d[:, :].rearrange("(q r p) t -> p q r t", r=2, p=128)
                            ).then_inc(dX, 16)
                e.dma_start(out=ids32[:, :], in_=id_d[:, :]).then_inc(dX, 16)
                if layer == 0:
                    e.dma_start(out=bo[:, :], in_=bo_d[:, :]).then_inc(dX, 16)

            @blk.gpsimd
            def _(e):
                for sl in range(2):
                    src = wo_d[:, sl * 512:(sl + 1) * 512].rearrange("(kc p) n -> p kc n", p=128)
                    e.dma_start(out=wsl[:, sl * 8192: sl * 8192 + 8192].rearrange("p (kc n) -> p kc n", n=512),
                                in_=src).then_inc(dA0 if sl % 2 == 0 else dA1, 16)

            @blk.vector
            def _(e):
                e.memset(eps[:, :], 1e-5)

        NPRE = (1 if load_x else 0) + 2 + (1 if layer == 0 else 0)
        cnt["dx"] = NPRE

        psA = ent(nc.psum_tensor("psA", [128, 2048], F32))
        with nc.Block() as blk:
            @blk.gpsimd
            def _(e):
                for sl in range(2, 4):
                    e.wait_ge(tMM, 8 * (sl - 1))
                    src = wo_d[:, sl * 512:(sl + 1) * 512].rearrange("(kc p) n -> p kc n", p=128)
                    e.dma_start(out=wsl[:, (sl % 2) * 8192:(sl % 2) * 8192 + 8192].rearrange("p (kc n) -> p kc n", n=512),
                                in_=src).then_inc(dA0 if sl % 2 == 0 else dA1, 16)

            @blk.tensor
            def _(e):
                e.wait_ge(dX, 16 * NPRE)
                for ocg in range(4):
                    e.wait_ge(dA0 if ocg % 2 == 0 else dA1, 16 * (ocg // 2 + 1))
                    for tt in range(8):
                        idx = ocg * 8 + tt
                        if idx >= 4:
                            e.wait_ge(vv, EV * (idx - 3))
                        for hc in range(16):
                            mm = e.matmul(psA[:, (idx % 4) * 512:(idx % 4) * 512 + 512],
                                          cxs[:, hc * TPC + tt * 128: hc * TPC + tt * 128 + 128],
                                          wsl[:, (ocg % 2) * 8192 + hc * 512:(ocg % 2) * 8192 + hc * 512 + 512],
                                          start=(hc == 0), stop=(hc == 15))
                            if hc == 15:
                                mm.then_inc(tMM, 1)

            @blk.vector
            def _(e):
                nv = 0
                for ocg in range(4):
                    for tt in range(8):
                        idx = ocg * 8 + tt
                        e.wait_ge(tMM, idx + 1)
                        xsl = xsb[:, tt * D + ocg * 512: tt * D + ocg * 512 + 512]
                        ps = psA[:, (idx % 4) * 512:(idx % 4) * 512 + 512]
                        if layer == 0:
                            e.wait_ge(vv, nv)
                            e.tensor_tensor(scr[:, :], ps, bo[:, ocg * 512:(ocg + 1) * 512],
                                            ALU.add).then_inc(vv, 1); nv += 1
                            e.wait_ge(vv, nv)
                            e.tensor_tensor(xsl, scr[:, :], xsl, ALU.add).then_inc(vv, 1); nv += 1
                        else:
                            e.wait_ge(vv, nv)
                            e.scalar_tensor_tensor(xsl, ps, 1.0, xsl,
                                                   ALU.mult, ALU.add).then_inc(vv, 1); nv += 1
                cnt["v"] = nv
        cnt["t"] = 32
        opctx.close()

        def ln_phase():
            V0, A0 = cnt["v"], cnt["a"]
            with nc.Block() as blk:
                @blk.scalar
                def _(e):
                    na = A0
                    for i in range(8):
                        e.wait_ge(aa, na)
                        if i >= 1:
                            e.wait_ge(vv, V0 + 7 * (i - 1) + 4)
                        e.activation(junk[:, :], xsb[:, i * D:(i + 1) * D], AF.Square,
                                     accum_out=sqs[:, :]).then_inc(aa, 1); na += 1
                        e.wait_ge(aa, na)
                        e.wait_ge(vv, V0 + 7 * i + 5)
                        e.activation(lnvb[:, :], varb[:, :], AF.Ln, bias=eps[:, :]).then_inc(aa, 1); na += 1
                        e.wait_ge(aa, na)
                        e.activation(rstd[:, :], lnvb[:, :], AF.Exp, scale=-0.5).then_inc(aa, 1); na += 1
                    cnt["a"] = na

                @blk.vector
                def _(e):
                    nv = V0

                    def vop(inst):
                        nonlocal nv
                        inst.then_inc(vv, 1); nv += 1

                    for i in range(8):
                        xt = xsb[:, i * D:(i + 1) * D]
                        e.wait_ge(vv, nv); vop(e.reduce_sum(srow[:, :], xt, axis=AX.X))
                        e.wait_ge(vv, nv); vop(e.tensor_scalar(mu[:, :], srow[:, :], 1.0 / 2048.0, None, ALU.mult, ALU.bypass))
                        e.wait_ge(vv, nv); vop(e.tensor_tensor(msq[:, :], mu[:, :], mu[:, :], ALU.mult))
                        e.wait_ge(vv, nv); e.wait_ge(aa, A0 + 3 * i + 1)
                        vop(e.tensor_scalar(e2b[:, :], sqs[:, :], 1.0 / 2048.0, None, ALU.mult, ALU.bypass))
                        e.wait_ge(vv, nv); vop(e.tensor_tensor(varb[:, :], e2b[:, :], msq[:, :], ALU.subtract))
                        e.wait_ge(vv, nv); e.wait_ge(aa, A0 + 3 * i + 3)
                        vop(e.tensor_tensor(mrs[:, :], mu[:, :], rstd[:, :], ALU.mult))
                        e.wait_ge(vv, nv); vop(e.tensor_scalar(xt, xt, rstd[:, :], mrs[:, :], ALU.mult, ALU.subtract))
                    cnt["v"] = nv

        ln_phase()

        if layer == 0:
            emit_moe(nc, ctx, cnt, dict(xsb=xsb, junk=junk, ids32=ids32, eps=eps,
                                        wg_d=wg_d, wgu_d=wgu_d, wdn_d=wdn_d,
                                        wsgu_d=wsgu_d, wsdn_d=wsdn_d, sel_d=sel_d,
                                        psA=psA, tMM=tMM, aa=aa, vv=vv, dX=dX,
                                        g_load=g_load, g_wait=g_wait, gld=gld, ent=ent))
        else:
            emit_bitnet(nc, ctx, cnt, dict(xsb=xsb, junk=junk, eps=eps,
                                           wq_d=wq_d, o1f_d=o1f_d, orf_d=orf_d, idb_d=idb_d,
                                           psA=psA, tMM=tMM, aa=aa, vv=vv, dX=dX,
                                           g_load=g_load, g_wait=g_wait, gld=gld, ent=ent))

        ln_phase()

        if store:
            with nc.Block() as blk:
                @blk.sync
                def _(e):
                    e.dma_start(out=xo_d[:, :].rearrange("(tt p) d -> p tt d", p=128),
                                in_=xsb[:, :].rearrange("p (tt d) -> p tt d", d=D)).then_inc(dS, 16)
                    e.wait_ge(dS, 16)


class _NcProxy:

    def __init__(self, nc):
        object.__setattr__(self, "_nc", nc)
        object.__setattr__(self, "_ctr", [0])

    def _u(self, name):
        self._ctr[0] += 1
        return "%s_u%d" % (name, self._ctr[0])

    def sbuf_tensor(self, name, *a, **k):
        return self._nc.sbuf_tensor(self._u(name), *a, **k)

    def psum_tensor(self, name, *a, **k):
        return self._nc.psum_tensor(self._u(name), *a, **k)

    def semaphore(self, name=None, num=None):
        return self._nc.semaphore(self._u(name or "s"), num)

    def __getattr__(self, a):
        return getattr(self._nc, a)


def build_fused():
    nc = bass.Bass("TRN2", target_bir_lowering=False)
    T = nc.dram_tensor
    x_d = T("x", [TPC, D], F32, kind="ExternalInput")
    y_d = T("y", [TPC, D], F32, kind="ExternalOutput")
    id_d = T("ident32", [128, 128], F32, kind="ExternalInput")
    idb_d = T("identb", [128, 128], BF16, kind="ExternalInput")
    ra_d = T("rope_a", [128, 4096], BF16, kind="ExternalInput")
    rb_d = T("rope_b", [128, 4096], BF16, kind="ExternalInput")
    mk_d = T("mask", [128, 896], BF16, kind="ExternalInput")
    iv_d = T("inv128", [128, 1], BF16, kind="ExternalInput")
    or_d = T("onesrow", [1, 128], BF16, kind="ExternalInput")
    o1f_d = T("ones128f", [128, 1], F32, kind="ExternalInput")
    orf_d = T("onesrowf", [1, 128], F32, kind="ExternalInput")
    sel_d = T("sel", [8, 1024], BF16, kind="ExternalInput")
    w0qkv_d = T("w0qkv", [D, 3 * D], F32, kind="ExternalInput")
    bq0_d = T("bqkv128", [128, 48], F32, kind="ExternalInput")
    bv0_d = T("bvbc", [128, D], F32, kind="ExternalInput")
    w0o_d = T("w0out", [D, D], F32, kind="ExternalInput")
    bo0_d = T("boutbc", [128, D], F32, kind="ExternalInput")
    wg_d = T("wgate", [D, 8], F32, kind="ExternalInput")
    wgu_d = T("wgu", [8, D, D], F32, kind="ExternalInput")
    wdn_d = T("wdn", [8, 1024, D], F32, kind="ExternalInput")
    wsgu_d = T("wsgu", [D, 8192], F32, kind="ExternalInput")
    wsdn_d = T("wsdn", [4096, D], F32, kind="ExternalInput")
    w1qkv_d = T("w1qkv", [D, 3 * D], F32, kind="ExternalInput")
    ls_d = T("lnsc", [128, 1], F32, kind="ExternalInput")
    lb_d = T("lnbi", [128, 1], F32, kind="ExternalInput")
    w1o_d = T("w1out", [D, D], F32, kind="ExternalInput")
    wq_d = T("wq", [D, D], F32, kind="ExternalInput")
    q_b = T("q_b", [2048, TPC], BF16)
    q_a = T("q_a", [2048, TPC], BF16)
    k_b = T("k_b", [2048, TPC], BF16)
    k_a = T("k_a", [2048, TPC], BF16)
    v_b = T("v_b", [8 * TPC, 256], BF16)
    v_g = T("v_g", [8 * TPC, 256], BF16)
    ctxa_in = T("ctxa_in", [2048, TPC], BF16)
    ctx_out = T("ctx_out", [2048, TPC], BF16)

    ncp = _NcProxy(nc)
    with contextlib.ExitStack() as ctx:
        ent = ctx.enter_context
        xsb = ent(nc.sbuf_tensor("xsb", [128, 8 * D], F32))
        dX0 = ent(nc.semaphore("dX0"))

        with nc.Block() as blk:
            @blk.sync
            def _(e):
                e.dma_start(out=xsb[:, :].rearrange("p (tt d) -> p tt d", d=D),
                            in_=x_d[:, :].rearrange("(tt p) d -> p tt d", p=128)).then_inc(dX0, 16)
                e.wait_ge(dX0, 16)

        cc = ent(nc.semaphore("cc"))
        ncc = 0
        for layer in range(2):
            mio = {"w_d": w0qkv_d if layer == 0 else w1qkv_d,
                   "ra_d": ra_d, "rb_d": rb_d, "id_d": id_d,
                   "qT_d": q_b, "kT_d": k_b, "v_d": v_b}
            if layer == 0:
                mio.update(bq_d=bq0_d, bv_d=bv0_d)
            else:
                mio.update(iv_d=iv_d, or_d=or_d, ls_d=ls_d, lb_d=lb_d)
            coll = {"cc": cc, "base": ncc, "q_in": q_b, "q_out": q_a,
                    "k_in": k_b, "k_out": k_a,
                    "v_in": v_b, "v_out": v_g, "ctx_in": ctxa_in, "ctx_out": ctx_out}
            emit_mixer(ncp, layer, xsb, mio, ent, load_x=False, coll=coll)

            aio = {"mk_d": mk_d, "id_d": id_d, "ctxa_d": ctxa_in,
                   "qa_d": q_a, "ka_d": k_a, "vg_d": v_g}
            emit_attn(ncp, layer, aio, ent, coll=coll)

            pio = {"wo_d": w0o_d if layer == 0 else w1o_d, "id_d": id_d,
                   "cxo_d": ctx_out, "xo_d": y_d}
            if layer == 0:
                pio.update(bo_d=bo0_d, wg_d=wg_d, wgu_d=wgu_d, wdn_d=wdn_d,
                           wsgu_d=wsgu_d, wsdn_d=wsdn_d, sel_d=sel_d)
            else:
                pio.update(wq_d=wq_d, o1f_d=o1f_d, orf_d=orf_d, idb_d=idb_d)
            emit_post(ncp, layer, xsb, pio, ent, load_x=False, store=(layer == 1), coll=coll)
            ncc += 4

    return nc


_PROG_CACHE = {}


def _prog(name, builder, *args):
    if name not in _PROG_CACHE:
        _PROG_CACHE[name] = builder(*args)
    return _PROG_CACHE[name]


def kernel(**inputs):
    z = inputs
    f64 = np.asarray(z["L0_attn_x"]).dtype == np.float64
    con = host_prep(f64)
    x_full = np.asarray(z["L0_attn_x"]).astype(np.float32, copy=False).reshape(2 * S, D)
    bq0 = np.asarray(z["L0_attn_bqkv"]).astype(np.float32, copy=False)

    nc = _prog("fused", build_fused)
    maps = []
    asf = lambda k: np.asarray(z[k]).astype(np.float32, copy=False)
    for core in range(8):
        ra, rb = rope_rep_for_core(con, core)
        m = {
            "x": np.ascontiguousarray(x_full[core * TPC:(core + 1) * TPC]),
            "ident32": con["ident32"], "identb": con["ident"],
            "rope_a": ra, "rope_b": rb, "mask": con["mask"],
            "inv128": con["inv128"], "onesrow": con["onesrow"],
            "ones128f": np.ones((128, 1), np.float32),
            "onesrowf": np.ones((1, 128), np.float32),
            "sel": con["sel"],
            "w0qkv": asf("L0_attn_Wqkv"),
            "bqkv128": np.ascontiguousarray(bq0.reshape(48, 128).T),
            "bvbc": np.ascontiguousarray(np.broadcast_to(bq0[4096:], (128, D))),
            "w0out": asf("L0_attn_Wout"),
            "boutbc": np.ascontiguousarray(np.broadcast_to(asf("L0_attn_bout"), (128, D))),
            "wgate": asf("L0_ffn_w_gate"), "wgu": asf("L0_ffn_w_gate_up"),
            "wdn": asf("L0_ffn_w_down"), "wsgu": asf("L0_ffn_w_sh_gate_up"),
            "wsdn": asf("L0_ffn_w_sh_down"),
            "w1qkv": asf("L1_attn_w_qkv"),
            "lnsc": np.ascontiguousarray(asf("L1_attn_ln_scale")[:, None]),
            "lnbi": np.ascontiguousarray(asf("L1_attn_ln_bias")[:, None]),
            "w1out": asf("L1_attn_w_out"),
            "wq": asf("L1_ffn_weight"),
        }
        maps.append(m)
    res = run_bass_kernel_spmd(nc, maps, core_ids=list(range(8)))
    y = np.concatenate([res.results[c]["y"] for c in range(8)], axis=0)
    return y.reshape(2, S, D).astype(np.float32, copy=False)


def kernel_unfused(**inputs):
    z = inputs
    f64 = np.asarray(z["L0_attn_x"]).dtype == np.float64
    con = host_prep(f64)
    x_full = np.asarray(z["L0_attn_x"]).astype(np.float32, copy=False).reshape(2 * S, D)

    def run(nc, maps):
        return run_bass_kernel_spmd(nc, maps, core_ids=list(range(8)))

    bq0 = np.asarray(z["L0_attn_bqkv"]).astype(np.float32, copy=False)

    out = np.empty((2 * S, D), np.float32)
    xcur = x_full
    for layer in range(2):
        wqkv = np.asarray(z["L0_attn_Wqkv"] if layer == 0 else z["L1_attn_w_qkv"]).astype(np.float32, copy=False)
        wout = np.asarray(z["L0_attn_Wout"] if layer == 0 else z["L1_attn_w_out"]).astype(np.float32, copy=False)

        nc = _prog("mix%d" % layer, build_mixer, layer)
        maps = []
        for core in range(8):
            ra, rb = rope_rep_for_core(con, core)
            m = {"x": np.ascontiguousarray(xcur[core * TPC:(core + 1) * TPC]),
                 "wqkv": wqkv, "rope_a": ra, "rope_b": rb, "ident32": con["ident32"]}
            if layer == 0:
                m["bqkv128"] = np.ascontiguousarray(bq0.reshape(48, 128).T)
                m["bvbc"] = np.ascontiguousarray(np.broadcast_to(bq0[4096:], (128, D)))
            else:
                m["inv128"] = con["inv128"]
                m["onesrow"] = con["onesrow"]
                m["lnsc"] = np.ascontiguousarray(np.asarray(z["L1_attn_ln_scale"]).astype(np.float32, copy=False)[:, None])
                m["lnbi"] = np.ascontiguousarray(np.asarray(z["L1_attn_ln_bias"]).astype(np.float32, copy=False)[:, None])
            maps.append(m)
        res = run(nc, maps)
        qk_shard = [res.results[c]["qkT"] for c in range(8)]
        v_full = np.concatenate([res.results[c]["vout"] for c in range(8)], axis=0)

        nc = _prog("att%d" % layer, build_attn, layer)
        maps = []
        for core in range(8):
            qka = np.concatenate([qk_shard[i][core * 512:(core + 1) * 512] for i in range(8)], axis=0)
            maps.append({"qka": np.ascontiguousarray(qka), "vg": v_full,
                         "mask": con["mask"], "ident32": con["ident32"]})
        res = run(nc, maps)
        ctx_shard = [res.results[c]["ctxa"] for c in range(8)]

        nc = _prog("post%d" % layer, build_post, layer)
        maps = []
        for core in range(8):
            cxo = np.concatenate([ctx_shard[i][core * 256:(core + 1) * 256] for i in range(8)], axis=0)
            m = {"x": np.ascontiguousarray(xcur[core * TPC:(core + 1) * TPC]),
                 "ctxo": np.ascontiguousarray(cxo), "wout": wout, "ident32": con["ident32"]}
            if layer == 0:
                m["boutbc"] = np.ascontiguousarray(np.broadcast_to(
                    np.asarray(z["L0_attn_bout"]).astype(np.float32, copy=False), (128, D)))
                m["wgate"] = np.asarray(z["L0_ffn_w_gate"]).astype(np.float32, copy=False)
                m["wgu"] = np.asarray(z["L0_ffn_w_gate_up"]).astype(np.float32, copy=False)
                m["wdn"] = np.asarray(z["L0_ffn_w_down"]).astype(np.float32, copy=False)
                m["wsgu"] = np.asarray(z["L0_ffn_w_sh_gate_up"]).astype(np.float32, copy=False)
                m["wsdn"] = np.asarray(z["L0_ffn_w_sh_down"]).astype(np.float32, copy=False)
                m["sel"] = con["sel"]
            else:
                m["wq"] = np.asarray(z["L1_ffn_weight"]).astype(np.float32, copy=False)
                m["ones128f"] = np.ones((128, 1), np.float32)
                m["onesrowf"] = np.ones((1, 128), np.float32)
                m["identb"] = con["ident"]
            maps.append(m)
        res = run(nc, maps)
        xcur = np.concatenate([res.results[c]["xo"] for c in range(8)], axis=0)

    return xcur.reshape(2, S, D).astype(np.float32, copy=False)


# version 14 (final)
# speedup vs baseline: 1.0923x; 1.0923x over previous
import contextlib
import numpy as np
import ml_dtypes
import concourse.bass as bass
import concourse.mybir as mybir
from concourse.bass_utils import run_bass_kernel_spmd
from concourse.aot_env import allow_ici_all_to_all

F32 = mybir.dt.float32
BF16 = mybir.dt.bfloat16
AF = mybir.ActivationFunctionType
ALU = mybir.AluOpType
AX = mybir.AxisListType
BF = ml_dtypes.bfloat16

D = 2048
H = 16
S = 4096
TPC = 1024
RC = 12582912.0
ISQ = float(1.0 / np.sqrt(np.float64(128.0)))


def host_prep(f64=False):
    dt = np.float64 if f64 else np.float32
    c = {}
    c["ident"] = np.eye(128, dtype=BF)
    c["ident32"] = np.eye(128, dtype=np.float32)
    c["inv128"] = np.full((128, 1), 1.0 / 128.0, dtype=BF)
    c["onesrow"] = np.ones((1, 128), dtype=BF)
    kv = np.arange(128)[:, None]
    cc = np.arange(896)[None, :]
    c["mask"] = (kv + 384 <= cc).astype(BF)
    sel = np.zeros((8, 8 * 128), dtype=BF)
    for e in range(8):
        sel[e, e * 128:(e + 1) * 128] = 1.0
    c["sel"] = sel
    inv = (1.0 / np.power(dt(10000.0), np.arange(0, 128, 2).astype(dt) / dt(128.0)))
    t = np.arange(S).astype(dt)
    f = np.outer(t, inv)
    c["_cosT"] = np.ascontiguousarray(np.cos(f).T.astype(np.float32))
    c["_sinT"] = np.ascontiguousarray(np.sin(f).T.astype(np.float32))
    return c


def rope_rep_for_core(c, core):
    s0 = (core % 4) * TPC
    ra = np.concatenate([c["_cosT"][:, s0:s0 + TPC], c["_sinT"][:, s0:s0 + TPC]], axis=0)
    rb = np.concatenate([c["_sinT"][:, s0:s0 + TPC], c["_cosT"][:, s0:s0 + TPC]], axis=0)
    ra_rep = np.concatenate([np.tile(ra[:, g * 512:(g + 1) * 512], (1, 4)) for g in range(2)], axis=1)
    rb_rep = np.concatenate([np.tile(rb[:, g * 512:(g + 1) * 512], (1, 4)) for g in range(2)], axis=1)
    return ra_rep.astype(BF), rb_rep.astype(BF)


def build_mixer(layer):
    nc = bass.Bass("TRN2", target_bir_lowering=False)
    io = {}
    io["x_d"] = nc.dram_tensor("x", [TPC, D], F32, kind="ExternalInput")
    io["w_d"] = nc.dram_tensor("wqkv", [D, 3 * D], F32, kind="ExternalInput")
    io["ra_d"] = nc.dram_tensor("rope_a", [128, 4096], BF16, kind="ExternalInput")
    io["rb_d"] = nc.dram_tensor("rope_b", [128, 4096], BF16, kind="ExternalInput")
    io["id_d"] = nc.dram_tensor("ident32", [128, 128], F32, kind="ExternalInput")
    if layer == 0:
        io["bq_d"] = nc.dram_tensor("bqkv128", [128, 48], F32, kind="ExternalInput")
        io["bv_d"] = nc.dram_tensor("bvbc", [128, D], F32, kind="ExternalInput")
    else:
        io["iv_d"] = nc.dram_tensor("inv128", [128, 1], BF16, kind="ExternalInput")
        io["or_d"] = nc.dram_tensor("onesrow", [1, 128], BF16, kind="ExternalInput")
        io["ls_d"] = nc.dram_tensor("lnsc", [128, 1], F32, kind="ExternalInput")
        io["lb_d"] = nc.dram_tensor("lnbi", [128, 1], F32, kind="ExternalInput")
    io["qT_d"] = nc.dram_tensor("qT", [2048, TPC], BF16, kind="ExternalOutput")
    io["kT_d"] = nc.dram_tensor("kT", [2048, TPC], BF16, kind="ExternalOutput")
    io["v_d"] = nc.dram_tensor("vout", [8 * TPC, 256], BF16, kind="ExternalOutput")
    with contextlib.ExitStack() as ctx:
        xsb = ctx.enter_context(nc.sbuf_tensor("xsb", [128, 8 * D], F32))
        emit_mixer(nc, layer, xsb, io, ctx.enter_context, load_x=True)
    return nc


def emit_mixer(nc, layer, xsb, io, sement, load_x, coll=None):
    x_d = io.get("x_d")
    w_d = io["w_d"]
    ra_d, rb_d, id_d = io["ra_d"], io["rb_d"], io["id_d"]
    if layer == 0:
        bq_d, bv_d = io["bq_d"], io["bv_d"]
    else:
        iv_d, or_d, ls_d, lb_d = io["iv_d"], io["or_d"], io["ls_d"], io["lb_d"]
    qT_d, kT_d, v_d = io["qT_d"], io["kT_d"], io["v_d"]

    with contextlib.ExitStack() as ctx:
        ent = ctx.enter_context
        xT = ent(nc.sbuf_tensor("xT", [128, 16 * TPC], BF16))
        wb = ent(nc.sbuf_tensor("wb", [128, 2 * 16 * 512], BF16))
        tmp = ent(nc.sbuf_tensor("tmp", [128, 2 * 2048], BF16))
        qko = ent(nc.sbuf_tensor("qko", [128, 2 * 2048], BF16))
        rsc = ent(nc.sbuf_tensor("rsc", [128, 2 * 2048], BF16))
        ra = ent(nc.sbuf_tensor("ra", [128, 4096], BF16))
        rb = ent(nc.sbuf_tensor("rb", [128, 4096], BF16))
        ids32 = ent(nc.sbuf_tensor("ids32", [128, 128], F32))
        if layer == 0:
            bq = ent(nc.sbuf_tensor("bq", [128, 48], F32))
            bv = ent(nc.sbuf_tensor("bv", [128, D], F32))
        else:
            iv = ent(nc.sbuf_tensor("iv", [128, 1], BF16))
            onr = ent(nc.sbuf_tensor("onr", [1, 128], BF16))
            lns = ent(nc.sbuf_tensor("lns", [128, 1], F32))
            lnb = ent(nc.sbuf_tensor("lnb", [128, 1], F32))
            eps = ent(nc.sbuf_tensor("eps", [128, 1], F32))
            sqb = ent(nc.sbuf_tensor("sqb", [128, 512], BF16))
            stm = ent(nc.sbuf_tensor("stm", [1, 512], BF16))
            str_ = ent(nc.sbuf_tensor("str", [1, 512], BF16))
            msf = ent(nc.sbuf_tensor("msf", [1, 512], F32))
            vrf = ent(nc.sbuf_tensor("vrf", [1, 512], F32))
            lnv = ent(nc.sbuf_tensor("lnv", [1, 512], F32))
            t1b = ent(nc.sbuf_tensor("t1b", [128, 512], BF16))
        dX = sement(nc.semaphore("dX"))
        dW0 = sement(nc.semaphore("dW0"))
        dW1 = sement(nc.semaphore("dW1"))
        dS0 = sement(nc.semaphore("dS0"))
        dS1 = sement(nc.semaphore("dS1"))
        tMM = sement(nc.semaphore("tMM"))
        tEV = sement(nc.semaphore("tEV"))
        vv = sement(nc.semaphore("vv"))
        tH = sement(nc.semaphore("tH"))

        def rope_done(i):
            return 6 * (i + 1)

        def v_done(i):
            return 96 + 4 * (i - 15)

        def sdp(i):
            if i < 0:
                return 0
            return sum(2 for j in range(i % 2, i + 1, 2))

        npre = (5 if layer == 0 else 7) + (1 if load_x else 0)

        with nc.psum_tensor("psT1", [128, 128], F32) as psT1, \
             nc.psum_tensor("psT2", [128, 128], F32) as psT2, \
             nc.Block() as blk:
            @blk.sync
            def _(e):
                if load_x:
                    e.dma_start(out=xsb[:, :].rearrange("p (tt d) -> p tt d", d=D),
                                in_=x_d[:, :].rearrange("(tt p) d -> p tt d", p=128)).then_inc(dX, 16)
                e.dma_start(out=ra[:, :], in_=ra_d[:, :]).then_inc(dX, 16)
                e.dma_start(out=rb[:, :], in_=rb_d[:, :]).then_inc(dX, 16)
                e.dma_start(out=ids32[:, :], in_=id_d[:, :]).then_inc(dX, 16)
                if layer == 0:
                    e.dma_start(out=bq[:, :], in_=bq_d[:, :]).then_inc(dX, 16)
                else:
                    e.dma_start(out=iv[:, :], in_=iv_d[:, :]).then_inc(dX, 16)
                    e.dma_start(out=onr[:, :], in_=or_d[:, :]).then_inc(dX, 16)
                if layer == 0:
                    e.dma_start(out=bv[:, :], in_=bv_d[:, :]).then_inc(dX, 16)
                else:
                    e.dma_start(out=lns[:, :], in_=ls_d[:, :]).then_inc(dX, 16)
                    e.dma_start(out=lnb[:, :], in_=lb_d[:, :]).then_inc(dX, 16)

            @blk.gpsimd
            def _(e):
                for st in range(2):
                    src = w_d[:, st * 512:(st + 1) * 512].rearrange("(kc p) n -> p kc n", p=128)
                    e.dma_start(out=wb[:, st * 8192: st * 8192 + 8192].rearrange("p (kc n) -> p kc n", n=512),
                                in_=src).then_inc(dW0 if st % 2 == 0 else dW1, 16)

            @blk.tensor
            def _(e):
                e.wait_ge(dX, 16 * npre)
                for i in range(128):
                    tt, dc = divmod(i, 16)
                    ps = psT1 if i % 2 == 0 else psT2
                    if i >= 2:
                        e.wait_ge(tEV, i - 1)
                    e.transpose(ps[:, :], xsb[:, tt * D + dc * 128: tt * D + dc * 128 + 128],
                                ids32[:, :]).then_inc(tMM, 1)

            @blk.vector
            def _(e):
                if layer == 1:
                    e.memset(eps[:, :], 1e-5)
                for i in range(128):
                    tt, dc = divmod(i, 16)
                    ps = psT1 if i % 2 == 0 else psT2
                    e.wait_ge(tMM, i + 1)
                    e.tensor_copy(xT[:, dc * TPC + tt * 128: dc * TPC + tt * 128 + 128],
                                  ps[:, :]).then_inc(tEV, 1)

        T0, E0 = 128, 128
        HS = 13

        def hafter(i):
            return HS * 4 * (i + 1)

        def make_emit_hT(psH1, psH2, psBC):
            def emit_hT(e, idx):
                for cc in range(4):
                    b0 = HS * (4 * idx + cc)
                    q = qko[:, (idx % 2) * 2048 + cc * 512:(idx % 2) * 2048 + cc * 512 + 512]
                    e.wait_ge(tH, b0)
                    e.wait_ge(vv, rope_done(idx))
                    e.matmul(psH1[0:1, :], iv[:, :], q, start=True, stop=True).then_inc(tH, 1)
                    e.wait_ge(tH, b0 + 2)
                    e.matmul(psH2[0:1, :], iv[:, :], sqb[:, :], start=True, stop=True).then_inc(tH, 1)
                    e.wait_ge(tH, b0 + 8)
                    e.matmul(psBC[:, 0:512], onr[:, :], stm[0:1, :], start=True, stop=True).then_inc(tH, 1)
                    e.wait_ge(tH, b0 + 9)
                    e.matmul(psBC[:, 512:1024], onr[:, :], str_[0:1, :], start=True, stop=True).then_inc(tH, 1)
            return emit_hT

        psA = ent(nc.psum_tensor("psA", [128, 2048], F32))
        if layer == 0:
            psB = ent(nc.psum_tensor("psB", [128, 2048], F32))
        else:
            psB = None
            psH1 = ent(nc.psum_tensor("psH1", [128, 512], F32))
            psH2 = ent(nc.psum_tensor("psH2", [128, 512], F32))
            psBC = ent(nc.psum_tensor("psBC", [128, 1024], F32))
            emit_hT = make_emit_hT(psH1, psH2, psBC)
        with nc.Block() as blk:
            @blk.gpsimd
            def _(e):
                def wload(st):
                    e.wait_ge(tMM, T0 + 2 * st - 2)
                    src = w_d[:, st * 512:(st + 1) * 512].rearrange("(kc p) n -> p kc n", p=128)
                    e.dma_start(out=wb[:, (st % 2) * 8192:(st % 2) * 8192 + 8192].rearrange("p (kc n) -> p kc n", n=512),
                                in_=src).then_inc(dW0 if st % 2 == 0 else dW1, 16)

                def a2a(key_in, key_out):
                    with allow_ici_all_to_all():
                        e.collective_compute("AllToAll", ALU.bypass,
                                             replica_groups=[list(range(8))],
                                             ins=[coll[key_in].ap().opt()],
                                             outs=[coll[key_out].ap().opt()]).then_inc(coll["cc"], 1)

                for st in range(2, 8):
                    wload(st)
                if coll is not None:
                    e.wait_ge(dS0, 16 * sdp(6))
                    e.wait_ge(dS1, 16 * sdp(7))
                    a2a("q_in", "q_out")
                for st in range(8, 10):
                    wload(st)
                if coll is not None:
                    e.wait_ge(dS0, 16 * sdp(14))
                    e.wait_ge(dS1, 16 * sdp(15))
                    a2a("k_in", "k_out")
                for st in range(10, 12):
                    wload(st)
                if coll is not None:
                    e.wait_ge(dS0, 16 * sdp(22))
                    e.wait_ge(dS1, 16 * sdp(23))
                    a2a("v_in", "v_out")

            @blk.tensor
            def _(e):
                for st in range(12):
                    e.wait_ge(dW0 if st % 2 == 0 else dW1, 16 * (st // 2 + 1))
                    wv = 8192 * (st % 2)
                    for g in range(2):
                        idx = st * 2 + g
                        if layer == 0:
                            ps = psA if idx % 2 == 0 else psB
                        else:
                            ps = psA
                        if layer == 0:
                            if 2 <= idx < 18:
                                e.wait_ge(tEV, E0 + 4 * (idx - 1))
                            elif idx >= 18:
                                e.wait_ge(vv, v_done(idx - 2))
                        else:
                            if 1 <= idx <= 16:
                                e.wait_ge(tEV, E0 + 4 * idx)
                            elif idx > 16:
                                e.wait_ge(vv, v_done(idx - 1))
                            if idx == 16:
                                e.wait_ge(tH, hafter(15))
                        if st < 8:
                            for cc in range(4):
                                for kc in range(16):
                                    mm = e.matmul(ps[:, cc * 512:(cc + 1) * 512],
                                                  wb[:, wv + kc * 512 + cc * 128: wv + kc * 512 + cc * 128 + 128],
                                                  xT[:, kc * TPC + g * 512: kc * TPC + g * 512 + 512],
                                                  start=(kc == 0), stop=(kc == 15))
                                    if cc == 3 and kc == 15:
                                        mm.then_inc(tMM, 1)
                            if layer == 1:
                                if idx >= 1:
                                    emit_hT(e, idx - 1)
                                if idx == 15:
                                    emit_hT(e, 15)
                        else:
                            for tt in range(4):
                                tok = g * 512 + tt * 128
                                for kc in range(16):
                                    mm = e.matmul(ps[:, tt * 512:(tt + 1) * 512],
                                                  xT[:, kc * TPC + tok: kc * TPC + tok + 128],
                                                  wb[:, wv + kc * 512: wv + kc * 512 + 512],
                                                  start=(kc == 0), stop=(kc == 15))
                                    if tt == 3 and kc == 15:
                                        mm.then_inc(tMM, 1)

            @blk.scalar
            def _(e):
                def emit_evac(idx):
                    st_, g_ = idx // 2, idx % 2
                    ps = (psA if idx % 2 == 0 else psB) if layer == 0 else psA
                    e.wait_ge(tMM, T0 + idx + 1)
                    if idx >= 2:
                        e.wait_ge(vv, rope_done(idx - 2))
                    for cc in range(4):
                        dst = tmp[:, (idx % 2) * 2048 + cc * 512:(idx % 2) * 2048 + cc * 512 + 512]
                        if layer == 0:
                            e.activation(dst, ps[:, cc * 512:(cc + 1) * 512], AF.Identity,
                                         bias=bq[:, st_ * 4 + cc: st_ * 4 + cc + 1]).then_inc(tEV, 1)
                        else:
                            e.activation(dst, ps[:, cc * 512:(cc + 1) * 512],
                                         AF.Identity).then_inc(tEV, 1)

                def emit_chainA(idx):
                    for cc in range(4):
                        b0 = HS * (4 * idx + cc)
                        q = qko[:, (idx % 2) * 2048 + cc * 512:(idx % 2) * 2048 + cc * 512 + 512]
                        e.wait_ge(tH, b0 + 1)
                        e.wait_ge(vv, rope_done(idx))
                        e.activation(sqb[:, :], q, AF.Square).then_inc(tH, 1)
                        e.wait_ge(tH, b0 + 6)
                        e.activation(lnv[0:1, :], vrf[0:1, :], AF.Ln, bias=eps[0:1, :]).then_inc(tH, 1)
                        e.wait_ge(tH, b0 + 7)
                        e.activation(str_[0:1, :], lnv[0:1, :], AF.Exp, scale=-0.5).then_inc(tH, 1)

                if layer == 0:
                    for idx in range(16):
                        emit_evac(idx)
                else:
                    emit_evac(0)
                    for idx in range(1, 16):
                        emit_evac(idx)
                        emit_chainA(idx - 1)
                    emit_chainA(15)

            @blk.vector
            def _(e):
                nv = 0

                def vop(inst):
                    nonlocal nv
                    inst.then_inc(vv, 1)
                    nv += 1
                    return inst

                for st in range(12):
                    for g in range(2):
                        idx = st * 2 + g
                        o = (idx % 2) * 2048
                        if st < 8:
                            e.wait_ge(tEV, E0 + 4 * (idx + 1))
                            if idx >= 2:
                                e.wait_ge(dS0 if idx % 2 == 0 else dS1, 16 * sdp(idx - 2))
                                if layer == 1:
                                    e.wait_ge(tH, hafter(idx - 2))
                            gg = g * 2048
                            e.wait_ge(vv, nv)
                            vop(e.tensor_tensor(rsc[0:64, 0:2048], tmp[0:64, o:o + 2048],
                                                ra[0:64, gg:gg + 2048], ALU.mult))
                            e.wait_ge(vv, nv)
                            vop(e.tensor_tensor(rsc[0:64, 2048:4096], tmp[64:128, o:o + 2048],
                                                ra[64:128, gg:gg + 2048], ALU.mult))
                            e.wait_ge(vv, nv)
                            vop(e.tensor_tensor(rsc[64:128, 0:2048], tmp[0:64, o:o + 2048],
                                                rb[0:64, gg:gg + 2048], ALU.mult))
                            e.wait_ge(vv, nv)
                            vop(e.tensor_tensor(rsc[64:128, 2048:4096], tmp[64:128, o:o + 2048],
                                                rb[64:128, gg:gg + 2048], ALU.mult))
                            e.wait_ge(vv, nv)
                            vop(e.tensor_tensor(qko[0:64, o:o + 2048], rsc[0:64, 0:2048],
                                                rsc[0:64, 2048:4096], ALU.subtract))
                            e.wait_ge(vv, nv)
                            vop(e.tensor_tensor(qko[64:128, o:o + 2048], rsc[64:128, 0:2048],
                                                rsc[64:128, 2048:4096], ALU.add))
                            if layer == 1:
                                for cc in range(4):
                                    b0 = HS * (4 * idx + cc)
                                    q = qko[:, o + cc * 512: o + cc * 512 + 512]
                                    e.wait_ge(tH, b0 + 3)
                                    e.tensor_copy(stm[0:1, :], psH1[0:1, :]).then_inc(tH, 1)
                                    e.wait_ge(tH, b0 + 4)
                                    e.tensor_tensor(msf[0:1, :], stm[0:1, :], stm[0:1, :], ALU.mult).then_inc(tH, 1)
                                    e.wait_ge(tH, b0 + 5)
                                    e.tensor_tensor(vrf[0:1, :], psH2[0:1, :], msf[0:1, :], ALU.subtract).then_inc(tH, 1)
                                    e.wait_ge(tH, b0 + 10)
                                    e.tensor_tensor(t1b[:, :], q, psBC[:, 0:512], ALU.subtract).then_inc(tH, 1)
                                    e.wait_ge(tH, b0 + 11)
                                    e.tensor_tensor(t1b[:, :], t1b[:, :], psBC[:, 512:1024], ALU.mult).then_inc(tH, 1)
                                    e.wait_ge(tH, b0 + 12)
                                    e.tensor_scalar(q, t1b[:, :], lns[:, :], lnb[:, :],
                                                    ALU.mult, ALU.add).then_inc(tH, 1)
                        else:
                            ps = (psA if idx % 2 == 0 else psB) if layer == 0 else psA
                            e.wait_ge(tMM, T0 + idx + 1)
                            if idx >= 2:
                                e.wait_ge(dS0 if idx % 2 == 0 else dS1, 16 * sdp(idx - 2))
                            for tt in range(4):
                                dst = qko[:, o + tt * 512: o + tt * 512 + 512]
                                e.wait_ge(vv, nv)
                                if layer == 0:
                                    vop(e.tensor_tensor(dst, ps[:, tt * 512:(tt + 1) * 512],
                                                        bv[:, (st - 8) * 512:(st - 7) * 512], ALU.add))
                                else:
                                    vop(e.tensor_copy(dst, ps[:, tt * 512:(tt + 1) * 512]))

            @blk.sync
            def _(e):
                for st in range(12):
                    for g in range(2):
                        idx = st * 2 + g
                        o = (idx % 2) * 2048
                        if st < 8:
                            if layer == 0:
                                e.wait_ge(vv, rope_done(idx))
                            else:
                                e.wait_ge(tH, hafter(idx))
                            dst = (qT_d if st < 4 else kT_d)[:, :].rearrange(
                                "(a sh c p) t -> p a sh c t", sh=2, c=2, p=128)
                            for sh in range(2):
                                e.dma_start(out=dst[:, st % 4, sh, :, g * 512:(g + 1) * 512],
                                            in_=qko[:, o + sh * 1024:o + sh * 1024 + 1024]
                                            .rearrange("p (c t) -> p c t", t=512)
                                            ).then_inc(dS0 if idx % 2 == 0 else dS1, 16)
                        else:
                            e.wait_ge(vv, v_done(idx))
                            dst = v_d[:, :].rearrange("(j tt p) c -> p j tt c", j=8, p=128)
                            for jh in range(2):
                                e.dma_start(out=dst[:, (st - 8) * 2 + jh, g * 4:g * 4 + 4, :],
                                            in_=qko[:, o:o + 2048]
                                            .rearrange("p (tt jh c) -> p tt jh c", jh=2, c=256)[:, :, jh, :]
                                            ).then_inc(dS0 if idx % 2 == 0 else dS1, 16)
                e.wait_ge(dS0, 16 * sdp(22))
                e.wait_ge(dS1, 16 * sdp(23))


def attn_schedule():
    sched = []
    for qg in range(8):
        nkv = 4 * qg + 4
        tiles = [(kvt, 128 * (kvt - 4 * qg) if kvt >= 4 * qg else None) for kvt in range(nkv)]
        groups = [tiles[i:i + 3] for i in range(0, nkv, 3)]
        for kg, gt in enumerate(groups):
            sched.append((qg, kg, gt, len(groups)))
    return sched


def build_attn(layer):
    nc = bass.Bass("TRN2", target_bir_lowering=False)
    io = {}
    io["qa_d"] = nc.dram_tensor("qa", [2048, TPC], BF16, kind="ExternalInput")
    io["ka_d"] = nc.dram_tensor("ka", [2048, TPC], BF16, kind="ExternalInput")
    io["vg_d"] = nc.dram_tensor("vg", [8 * TPC, 256], BF16, kind="ExternalInput")
    io["mk_d"] = nc.dram_tensor("mask", [128, 896], BF16, kind="ExternalInput")
    io["id_d"] = nc.dram_tensor("ident32", [128, 128], F32, kind="ExternalInput")
    io["ctxa_d"] = nc.dram_tensor("ctxa", [2048, TPC], BF16, kind="ExternalOutput")
    with contextlib.ExitStack() as ctx:
        emit_attn(nc, layer, io, ctx.enter_context)
    return nc


def emit_attn(nc, layer, io, sement, coll=None):
    mk_d, id_d, ctxa_d = io["mk_d"], io["id_d"], io["ctxa_d"]

    sched = attn_schedule()
    NG = len(sched)

    VG = {}
    VR = {}
    VC = {}
    PREV = {}
    nv = 0
    last_par = {0: None, 1: None}
    pendp = None
    for u in range(4):
        for gi, (qg, kg, gt, ngq) in enumerate(sched):
            i = u * len(sched) + gi
            PREV[i] = last_par[gi % 2]
            last_par[gi % 2] = i
            nv += (1 if layer == 1 else 0) + sum(1 for _, mR in gt if mR is not None)
            VG[(u, gi)] = nv
            if kg == ngq - 1:
                if pendp is not None:
                    nv += 8
                    VR[pendp] = nv
                    nv += 4
                    VC[pendp] = nv
                pendp = (u, qg)
    nv += 8
    VR[pendp] = nv
    nv += 4
    VC[pendp] = nv

    with contextlib.ExitStack() as ctx:
        ent = ctx.enter_context
        qb = ent(nc.sbuf_tensor("qb", [128, 2 * S], BF16))
        kb = ent(nc.sbuf_tensor("kb", [128, 2 * S], BF16))
        vb = ent(nc.sbuf_tensor("vb", [128, 4 * 32 * 129], BF16))
        pt = ent(nc.sbuf_tensor("pt", [128, (28 + 32) * 512], BF16))
        PT_OFF = [0, 28 * 512]
        sq2 = ent(nc.sbuf_tensor("sq2", [128, 2 * 1536], BF16))
        msk = ent(nc.sbuf_tensor("msk", [128, 896], BF16))
        ids32 = ent(nc.sbuf_tensor("ids32", [128, 128], F32))
        rcp = ent(nc.sbuf_tensor("rcp", [128, 1], F32))
        cxf = ent(nc.sbuf_tensor("cxf", [128, 4 * 128], F32))
        cxT = ent(nc.sbuf_tensor("cxT", [128, 1024], BF16))
        psS1 = ent(nc.psum_tensor("psS1", [128, 1536], F32))
        psS2 = ent(nc.psum_tensor("psS2", [128, 1536], F32))
        psC = ent(nc.psum_tensor("psC", [128, 1024], F32))
        dC = sement(nc.semaphore("dC"))
        dU0 = sement(nc.semaphore("dU0"))
        dU1 = sement(nc.semaphore("dU1"))
        dV0 = sement(nc.semaphore("dV0"))
        dV1 = sement(nc.semaphore("dV1"))
        dS = sement(nc.semaphore("dS"))
        sS = sement(nc.semaphore("sS"))
        sA = sement(nc.semaphore("sA"))
        sV = sement(nc.semaphore("sV"))
        sT = sement(nc.semaphore("sT"))
        vv = sement(nc.semaphore("vv"))

        def qgi(u, qg):
            return u * 8 + qg

        with nc.Block() as blk:
            @blk.sync
            def _(e):
                e.dma_start(out=msk[:, :], in_=mk_d[:, :]).then_inc(dC, 16)
                e.dma_start(out=ids32[:, :], in_=id_d[:, :]).then_inc(dC, 16)

            @blk.vector
            def _(e):
                e.memset(vb[:, :], 1.0)

        DPU = 2
        qa_d, ka_d, vg_d = io["qa_d"], io["ka_d"], io["vg_d"]
        qav = qa_d[:, :].rearrange("(i hl p) t -> p i hl t", hl=2, p=128)
        kav = ka_d[:, :].rearrange("(i hl p) t -> p i hl t", hl=2, p=128)
        vbv = vb[:, :].rearrange("p (hl bb c d) -> p hl bb c d", hl=2, bb=2, d=129)
        with nc.Block() as blk:
            @blk.gpsimd
            def _(e):
                if coll is not None:
                    e.wait_ge(coll["cc"], coll["base"] + 2)
                for u in range(4):
                    if u >= 2:
                        e.wait_ge(dS, 64 * (u - 1))
                    o = (u % 2)
                    b, hl = u % 2, u // 2
                    dU = dU0 if u % 2 == 0 else dU1
                    e.dma_start(out=qb[:, o * S:(o + 1) * S].rearrange("p (j t) -> p j t", t=1024),
                                in_=qav[:, 4 * b:4 * b + 4, hl, :]).then_inc(dU, 16)
                    e.dma_start(out=kb[:, o * S:(o + 1) * S].rearrange("p (j t) -> p j t", t=1024),
                                in_=kav[:, 4 * b:4 * b + 4, hl, :]).then_inc(dU, 16)
                    if u == 1:
                        if coll is not None:
                            e.wait_ge(coll["cc"], coll["base"] + 3)
                        for b2 in range(2):
                            for hl2 in range(2):
                                e.dma_start(out=vbv[:, hl2, b2, :, 0:128],
                                            in_=vg_d[b2 * S:(b2 + 1) * S, :]
                                            .rearrange("(c p) (hl d) -> p hl c d", p=128, hl=2)[:, hl2]
                                            ).then_inc(dV0 if b2 == 0 else dV1, 16)
                if coll is not None:
                    e.wait_ge(dS, 256)
                    with allow_ici_all_to_all():
                        e.collective_compute("AllToAll", ALU.bypass,
                                             replica_groups=[list(range(8))],
                                             ins=[coll["ctx_in"].ap().opt()],
                                             outs=[coll["ctx_out"].ap().opt()]).then_inc(coll["cc"], 1)

            @blk.tensor
            def _(e):
                def emit_pv(u, qg, gi_last):
                    o = u
                    qq = qgi(u, qg)
                    if qq > 0:
                        pq = (u, qg - 1) if qg > 0 else (u - 1, 7)
                        e.wait_ge(vv, VC[pq])
                    e.wait_ge(sA, u * NG + gi_last + 1)
                    e.wait_ge(vv, VG[(u, gi_last)])
                    nkv = 4 * qg + 4
                    po = PT_OFF[qg % 2]
                    for qs in range(4):
                        for kvt in range(nkv):
                            mm = e.matmul(psC[:, qs * 256: qs * 256 + 129],
                                          pt[:, po + kvt * 512 + qs * 128:
                                             po + kvt * 512 + qs * 128 + 128],
                                          vb[:, o * 4128 + kvt * 129: o * 4128 + kvt * 129 + 129],
                                          start=(kvt == 0), stop=(kvt == nkv - 1))
                            if qs == 3 and kvt == nkv - 1:
                                mm.then_inc(sV, 1)
                    e.wait_ge(vv, VR[(u, qg)])
                    for qs in range(4):
                        e.transpose(psC[:, qs * 256: qs * 256 + 128],
                                    cxf[:, qs * 128:(qs + 1) * 128], ids32[:, :]).then_inc(sT, 1)

                pend = None
                for u in range(4):
                    if u == 0:
                        e.wait_ge(dC, 32)
                    e.wait_ge(dU0 if u % 2 == 0 else dU1, 16 * DPU * (u // 2 + 1))
                    e.wait_ge(dV0 if u % 2 == 0 else dV1, 32)
                    o = (u % 2)
                    for gi, (qg, kg, gt, ngq) in enumerate(sched):
                        i = u * NG + gi
                        psS = psS1 if gi % 2 == 0 else psS2
                        if PREV[i] is not None:
                            e.wait_ge(sA, PREV[i] + 1)
                        for j, (kvt, mR) in enumerate(gt):
                            mm = e.matmul(psS[:, j * 512:(j + 1) * 512],
                                          kb[:, o * S + kvt * 128: o * S + kvt * 128 + 128],
                                          qb[:, o * S + qg * 512: o * S + qg * 512 + 512],
                                          start=True, stop=True)
                            if j == len(gt) - 1:
                                mm.then_inc(sS, 1)
                        if kg == ngq - 1:
                            if pend is not None:
                                emit_pv(*pend)
                            pend = (u, qg, gi)
                if pend is not None:
                    emit_pv(*pend)

            @blk.scalar
            def _(e):
                for u in range(4):
                    for gi, (qg, kg, gt, ngq) in enumerate(sched):
                        i = u * NG + gi
                        psS = psS1 if gi % 2 == 0 else psS2
                        e.wait_ge(sS, i + 1)
                        w = 512 * len(gt)
                        po = PT_OFF[qg % 2] + gt[0][0] * 512
                        if qg >= 2:
                            e.wait_ge(sV, qgi(u, qg) - 1)
                        elif u >= 1 and qg < 2:
                            e.wait_ge(sV, qgi(u, qg) - 1)
                        if layer == 0:
                            e.activation(pt[:, po:po + w], psS[:, 0:w], AF.Exp,
                                         scale=ISQ).then_inc(sA, 1)
                        else:
                            if PREV[i] is not None:
                                pi = PREV[i]
                                e.wait_ge(vv, VG[(pi // NG, pi % NG)])
                            e.activation(sq2[:, (gi % 2) * 1536:(gi % 2) * 1536 + w],
                                         psS[:, 0:w], AF.Square).then_inc(sA, 1)

            @blk.vector
            def _(e):
                nv2 = 0

                def vop(inst):
                    nonlocal nv2
                    inst.then_inc(vv, 1)
                    nv2 += 1
                    return inst

                def emit_rc(u_, qg_):
                    nonlocal nv2
                    e.wait_ge(sV, qgi(u_, qg_) + 1)
                    if qg_ >= 2:
                        e.wait_ge(dS, 64 * u_ + 16 * (qg_ // 2))
                    elif u_ >= 1:
                        e.wait_ge(dS, 64 * u_)
                    for qs in range(4):
                        e.wait_ge(vv, nv2)
                        vop(e.reciprocal(rcp[:, :], psC[:, qs * 256 + 128: qs * 256 + 129]))
                        e.wait_ge(vv, nv2)
                        vop(e.tensor_scalar(cxf[:, qs * 128:(qs + 1) * 128],
                                            psC[:, qs * 256: qs * 256 + 128],
                                            rcp[:, :], None, ALU.mult, ALU.bypass))
                    assert nv2 == VR[(u_, qg_)]
                    e.wait_ge(sT, 4 * (qgi(u_, qg_) + 1))
                    for qs in range(4):
                        e.wait_ge(vv, nv2)
                        vop(e.tensor_copy(cxT[:, (qg_ % 2) * 512 + qs * 128: (qg_ % 2) * 512 + qs * 128 + 128],
                                          psC[:, qs * 256: qs * 256 + 128]))
                    assert nv2 == VC[(u_, qg_)]

                pendD = None
                for u in range(4):
                    for gi, (qg, kg, gt, ngq) in enumerate(sched):
                        i = u * NG + gi
                        e.wait_ge(sA, i + 1)
                        po = PT_OFF[qg % 2] + gt[0][0] * 512
                        w = 512 * len(gt)
                        if kg == 0 and qgi(u, qg) >= 2:
                            e.wait_ge(sV, qgi(u, qg) - 1)
                        if layer == 1:
                            e.wait_ge(vv, nv2)
                            vop(e.tensor_tensor(pt[:, po:po + w],
                                                sq2[:, (gi % 2) * 1536:(gi % 2) * 1536 + w],
                                                sq2[:, (gi % 2) * 1536:(gi % 2) * 1536 + w], ALU.mult))
                        for j, (kvt, mR) in enumerate(gt):
                            if mR is not None:
                                e.wait_ge(vv, nv2)
                                vop(e.tensor_tensor(pt[:, PT_OFF[qg % 2] + kvt * 512:
                                                       PT_OFF[qg % 2] + kvt * 512 + 512],
                                                    pt[:, PT_OFF[qg % 2] + kvt * 512:
                                                       PT_OFF[qg % 2] + kvt * 512 + 512],
                                                    msk[:, 384 - mR: 384 - mR + 512], ALU.mult))
                        assert nv2 == VG[(u, gi)], (u, gi, nv2, VG[(u, gi)])
                        if kg == ngq - 1:
                            if pendD is not None:
                                emit_rc(*pendD)
                            pendD = (u, qg)
                if pendD is not None:
                    emit_rc(*pendD)

            @blk.sync
            def _(e):
                ctxav = ctxa_d[:, :].rearrange("(i hl p) t -> p i hl t", hl=2, p=128)
                for u in range(4):
                    b, hl = u % 2, u // 2
                    for jj in range(4):
                        e.wait_ge(vv, VC[(u, 2 * jj + 1)])
                        e.dma_start(out=ctxav[:, 4 * b + jj, hl, :],
                                    in_=cxT[:, 0:1024]).then_inc(dS, 16)
                e.wait_ge(dS, 256)


def emit_moe(nc, ctx, cnt, env):
    ent = env["ent"]
    xsb, junk, psA = env["xsb"], env["junk"], env["psA"]
    tMM, aa, vv, dX = env["tMM"], env["aa"], env["vv"], env["dX"]
    g_load, g_wait, gld = env["g_load"], env["g_wait"], env["gld"]
    wg_d, wgu_d, wdn_d = env["wg_d"], env["wgu_d"], env["wdn_d"]
    wsgu_d, wsdn_d, sel_d = env["wsgu_d"], env["wsdn_d"], env["sel_d"]
    ids32 = env["ids32"]

    xT = ent(nc.sbuf_tensor("xTm", [128, 2 * 16 * 512], BF16))
    wg8 = ent(nc.sbuf_tensor("wg8", [128, 16 * 8], BF16))
    gsl = ent(nc.sbuf_tensor("gsl", [128, 4 * 4096], BF16))
    dsl = ent(nc.sbuf_tensor("dsl", [128, 2 * 4096], BF16))
    actb = ent(nc.sbuf_tensor("actb", [128, 2 * 8 * 512], BF16))
    stmp = ent(nc.sbuf_tensor("stmp", [128, 2 * 512], BF16))
    lgt = ent(nc.sbuf_tensor("lgt", [128, 8], F32))
    prb = ent(nc.sbuf_tensor("prb", [128, 8], F32))
    prc = ent(nc.sbuf_tensor("prc", [128, 8], F32))
    kpb = ent(nc.sbuf_tensor("kpb", [128, 8], F32))
    cmbf = ent(nc.sbuf_tensor("cmbf", [128, 2 * 4 * 8], F32))
    m1b = ent(nc.sbuf_tensor("m1b", [128, 1], F32))
    m2b = ent(nc.sbuf_tensor("m2b", [128, 1], F32))
    nmx = ent(nc.sbuf_tensor("nmx", [128, 1], F32))
    rs1 = ent(nc.sbuf_tensor("rs1", [128, 1], F32))
    rc1 = ent(nc.sbuf_tensor("rc1", [128, 1], F32))
    dn2 = ent(nc.sbuf_tensor("dn2", [128, 1], F32))
    rc2 = ent(nc.sbuf_tensor("rc2", [128, 1], F32))
    psEO = [ent(nc.psum_tensor("psEO%d" % i, [128, 512], F32)) for i in range(4)]

    wg8_li = gld["i"]
    with nc.Block() as blk:
        @blk.gpsimd
        def _(e):
            g_load(e, wg8[:, :].rearrange("p (kc n) -> p kc n", n=8),
                   wg_d[:, :].rearrange("(kc p) n -> p kc n", p=128))
    NDX = cnt["dx"]

    T0, V0 = cnt["t"], cnt["v"]
    with nc.Block() as blk:
        @blk.tensor
        def _(e):
            for i in range(128):
                g, r = divmod(i, 64)
                tt, dc = divmod(r, 16)
                if i >= 2:
                    e.wait_ge(vv, V0 + i - 1)
                e.transpose(psEO[i % 2][:, 0:128],
                            xsb[:, (g * 4 + tt) * D + dc * 128: (g * 4 + tt) * D + dc * 128 + 128],
                            ids32[:, :]).then_inc(tMM, 1)

        @blk.vector
        def _(e):
            for i in range(128):
                g, r = divmod(i, 64)
                tt, dc = divmod(r, 16)
                e.wait_ge(tMM, T0 + i + 1)
                e.wait_ge(vv, V0 + i)
                e.tensor_copy(xT[:, g * 8192 + dc * 512 + tt * 128: g * 8192 + dc * 512 + tt * 128 + 128],
                              psEO[i % 2][:, 0:128]).then_inc(vv, 1)
    cnt["t"] += 128
    cnt["v"] += 128

    segs = []
    base = {"t": cnt["t"], "v": cnt["v"], "a": cnt["a"]}
    lcur = gld["i"]

    def addseg(kind, **kw):
        nonlocal lcur
        kw["T0"], kw["V0"], kw["A0"], kw["li0"] = base["t"], base["v"], base["a"], lcur
        segs.append((kind, kw))
        if kind == "router":
            base["t"] += 4; base["v"] += 60; base["a"] += 4
        elif kind == "gu":
            base["t"] += 16; base["v"] += 16; base["a"] += 16
            lcur += 8
        else:
            base["t"] += 32; base["v"] += 32
            lcur += 4

    addseg("router", g=0)
    addseg("router", g=1)
    for su in range(4):
        addseg("gu", first=(su == 0),
               srcs_of=lambda s, su=su: (
                   wsgu_d[:, su * 1024 + s * 256: su * 1024 + (s + 1) * 256].rearrange("(kc p) n -> p kc n", p=128),
                   wsgu_d[:, 4096 + su * 1024 + s * 256: 4096 + su * 1024 + (s + 1) * 256].rearrange("(kc p) n -> p kc n", p=128)))
        addseg("eo", first=(su == 0), ex=None,
               slab_src_of=lambda oc, su=su: wsdn_d[su * 1024:(su + 1) * 1024, oc * 512:(oc + 1) * 512]
               .rearrange("(kc p) n -> p kc n", p=128))
    for ex in range(8):
        addseg("gu", first=False,
               srcs_of=lambda s, ex=ex: (
                   wgu_d[ex, :, s * 256:(s + 1) * 256].rearrange("(kc p_) n -> p_ kc n", p_=128),
                   wgu_d[ex, :, 1024 + s * 256: 1024 + (s + 1) * 256].rearrange("(kc p_) n -> p_ kc n", p_=128)))
        addseg("eo", first=False, ex=ex,
               slab_src_of=lambda oc, ex=ex: wdn_d[ex, :, oc * 512:(oc + 1) * 512]
               .rearrange("(ic p_) n -> p_ ic n", p_=128))

    with nc.Block() as blk:
        @blk.gpsimd
        def _(e):
            for kind, kw in segs:
                T0 = kw["T0"]
                if kind == "gu":
                    for s in range(4):
                        if s >= 2:
                            e.wait_ge(tMM, T0 + 4 * (s - 1))
                        elif not kw["first"]:
                            e.wait_ge(tMM, T0 - 36 + 4 * s)
                        srcg, srcu = kw["srcs_of"](s)
                        slot = (s % 2) * 2
                        g_load(e, gsl[:, slot * 4096:(slot + 1) * 4096].rearrange("p (kc n) -> p kc n", n=256), srcg)
                        g_load(e, gsl[:, (slot + 1) * 4096:(slot + 2) * 4096].rearrange("p (kc n) -> p kc n", n=256), srcu)
                elif kind == "eo":
                    for oc in range(4):
                        if oc >= 2:
                            e.wait_ge(tMM, T0 + 8 * (oc - 1))
                        elif not kw["first"]:
                            e.wait_ge(tMM, T0 - 24 + 8 * oc)
                        g_load(e, dsl[:, (oc % 2) * 4096:(oc % 2 + 1) * 4096].rearrange("p (kc n) -> p kc n", n=512),
                               kw["slab_src_of"](oc))

        @blk.tensor
        def _(e):
            for kind, kw in segs:
                T0, V0, li0 = kw["T0"], kw["V0"], kw["li0"]
                if kind == "router":
                    g = kw["g"]
                    for tt in range(4):
                        if g == 0 and tt == 0:
                            e.wait_ge(dX, 16 * NDX)
                            g_wait(e, wg8_li)
                        if g == 1:
                            e.wait_ge(vv, V0 - 59 + 15 * tt)
                        for hc in range(16):
                            mm = e.matmul(psA[:, tt * 512: tt * 512 + 8],
                                          xT[:, g * 8192 + hc * 512 + tt * 128: g * 8192 + hc * 512 + tt * 128 + 128],
                                          wg8[:, hc * 8: hc * 8 + 8],
                                          start=(hc == 0), stop=(hc == 15))
                            if hc == 15:
                                mm.then_inc(tMM, 1)
                elif kind == "gu":
                    for s in range(4):
                        g_wait(e, li0 + 2 * s)
                        g_wait(e, li0 + 2 * s + 1)
                        slot = (s % 2) * 2
                        for tg in range(2):
                            h = 2 * s + tg
                            if h >= 1:
                                e.wait_ge(vv, V0 + 2 * h)
                            elif kw["first"]:
                                e.wait_ge(vv, V0)
                            else:
                                e.wait_ge(vv, V0 - 32)
                            for ci in range(4):
                                sl = slot + ci // 2
                                for hc in range(16):
                                    mm = e.matmul(psA[:, ci * 512:(ci + 1) * 512],
                                                  gsl[:, sl * 4096 + hc * 256 + (ci % 2) * 128:
                                                      sl * 4096 + hc * 256 + (ci % 2) * 128 + 128],
                                                  xT[:, tg * 8192 + hc * 512: tg * 8192 + hc * 512 + 512],
                                                  start=(hc == 0), stop=(hc == 15))
                                    if ci % 2 == 1 and hc == 15:
                                        mm.then_inc(tMM, 1)
                else:
                    for oc in range(4):
                        g_wait(e, li0 + oc)
                        slot = (oc % 2) * 4096
                        for g in range(2):
                            for tt in range(4):
                                if g == 1:
                                    e.wait_ge(vv, V0 + 8 * oc + tt + 1)
                                elif oc >= 1:
                                    e.wait_ge(vv, V0 + 8 * (oc - 1) + 4 + tt + 1)
                                else:
                                    e.wait_ge(vv, V0)
                                for ic in range(8):
                                    mm = e.matmul(psEO[tt][:, 0:512],
                                                  actb[:, (g * 8 + ic) * 512 + tt * 128: (g * 8 + ic) * 512 + tt * 128 + 128],
                                                  dsl[:, slot + ic * 512: slot + ic * 512 + 512],
                                                  start=(ic == 0), stop=(ic == 7))
                                    if ic == 7:
                                        mm.then_inc(tMM, 1)

        @blk.scalar
        def _(e):
            for kind, kw in segs:
                T0, V0, A0 = kw["T0"], kw["V0"], kw["A0"]
                if kind == "router":
                    for tt in range(4):
                        e.wait_ge(aa, A0 + tt)
                        e.wait_ge(vv, V0 + 15 * tt + 3)
                        e.activation(prb[:, :], lgt[:, :], AF.Exp, bias=nmx[:, :]).then_inc(aa, 1)
                elif kind == "gu":
                    for h in range(8):
                        for j in range(2):
                            e.wait_ge(aa, A0 + 2 * h + j)
                            e.wait_ge(tMM, T0 + 2 * h + 1)
                            if h >= 1:
                                e.wait_ge(vv, V0 + 2 * (h - 1) + j + 1)
                            elif not kw["first"]:
                                e.wait_ge(vv, V0 - 32)
                            e.activation(stmp[:, j * 512:(j + 1) * 512],
                                         psA[:, j * 512:(j + 1) * 512], AF.Silu).then_inc(aa, 1)

        @blk.vector
        def _(e):
            for kind, kw in segs:
                T0, V0, A0 = kw["T0"], kw["V0"], kw["A0"]
                nv = V0
                if kind == "router":
                    g = kw["g"]

                    def vop(inst):
                        nonlocal nv
                        inst.then_inc(vv, 1); nv += 1

                    for tt in range(4):
                        e.wait_ge(tMM, T0 + tt + 1)
                        e.wait_ge(vv, nv); vop(e.tensor_copy(lgt[:, :], psA[:, tt * 512: tt * 512 + 8]))
                        e.wait_ge(vv, nv); vop(e.reduce_max(m1b[:, :], lgt[:, :], axis=AX.X))
                        e.wait_ge(vv, nv); vop(e.tensor_scalar(nmx[:, :], m1b[:, :], -1.0, None, ALU.mult, ALU.bypass))
                        e.wait_ge(aa, A0 + tt + 1)
                        e.wait_ge(vv, nv); vop(e.reduce_sum(rs1[:, :], prb[:, :], axis=AX.X))
                        e.wait_ge(vv, nv); vop(e.reciprocal(rc1[:, :], rs1[:, :]))
                        e.wait_ge(vv, nv); vop(e.tensor_scalar(prc[:, :], prb[:, :], rc1[:, :], None, ALU.mult, ALU.bypass))
                        e.wait_ge(vv, nv); vop(e.reduce_max(m1b[:, :], prc[:, :], axis=AX.X))
                        e.wait_ge(vv, nv); vop(e.tensor_scalar(kpb[:, :], prc[:, :], m1b[:, :], None, ALU.is_ge, ALU.bypass))
                        e.wait_ge(vv, nv); vop(e.scalar_tensor_tensor(prb[:, :], kpb[:, :], -1e9, prc[:, :], ALU.mult, ALU.add))
                        e.wait_ge(vv, nv); vop(e.reduce_max(m2b[:, :], prb[:, :], axis=AX.X))
                        e.wait_ge(vv, nv); vop(e.tensor_scalar(kpb[:, :], prc[:, :], m2b[:, :], None, ALU.is_ge, ALU.bypass))
                        e.wait_ge(vv, nv); vop(e.tensor_tensor(dn2[:, :], m1b[:, :], m2b[:, :], ALU.add))
                        e.wait_ge(vv, nv); vop(e.reciprocal(rc2[:, :], dn2[:, :]))
                        e.wait_ge(vv, nv); vop(e.tensor_tensor(cmbf[:, g * 32 + tt * 8: g * 32 + (tt + 1) * 8], prc[:, :], kpb[:, :], ALU.mult))
                        e.wait_ge(vv, nv); vop(e.tensor_scalar(cmbf[:, g * 32 + tt * 8: g * 32 + (tt + 1) * 8],
                                                               cmbf[:, g * 32 + tt * 8: g * 32 + (tt + 1) * 8],
                                                               rc2[:, :], None, ALU.mult, ALU.bypass))
                    assert nv == V0 + 60, nv
                elif kind == "gu":
                    for s in range(4):
                        for tg in range(2):
                            h = 2 * s + tg
                            for j in range(2):
                                e.wait_ge(aa, A0 + 2 * h + j + 1)
                                e.wait_ge(tMM, T0 + 2 * h + 2)
                                ao = (tg * 8 + 2 * s + j) * 512
                                e.wait_ge(vv, nv)
                                e.tensor_tensor(actb[:, ao: ao + 512],
                                                stmp[:, j * 512:(j + 1) * 512],
                                                psA[:, (2 + j) * 512:(3 + j) * 512], ALU.mult).then_inc(vv, 1)
                                nv += 1
                else:
                    ex = kw["ex"]
                    for oc in range(4):
                        for g in range(2):
                            for tt in range(4):
                                e.wait_ge(tMM, T0 + 8 * oc + 4 * g + tt + 1)
                                xsl = xsb[:, (g * 4 + tt) * D + oc * 512:
                                          (g * 4 + tt) * D + oc * 512 + 512]
                                e.wait_ge(vv, nv)
                                if ex is None:
                                    e.tensor_tensor(xsl, psEO[tt][:, 0:512], xsl, ALU.add).then_inc(vv, 1)
                                else:
                                    e.scalar_tensor_tensor(xsl, psEO[tt][:, 0:512],
                                                           cmbf[:, g * 32 + tt * 8 + ex: g * 32 + tt * 8 + ex + 1],
                                                           xsl, ALU.mult, ALU.add).then_inc(vv, 1)
                                nv += 1

    cnt["t"], cnt["v"], cnt["a"] = base["t"], base["v"], base["a"]
    assert gld["i"] == lcur


def emit_bitnet(nc, ctx, cnt, env):
    ent = env["ent"]
    xsb, junk, psA = env["xsb"], env["junk"], env["psA"]
    tMM, aa, vv, dX = env["tMM"], env["aa"], env["vv"], env["dX"]
    g_load, g_wait, gld = env["g_load"], env["g_wait"], env["gld"]
    wq_d, o1f_d, orf_d, idb_d = env["wq_d"], env["o1f_d"], env["orf_d"], env["idb_d"]
    eps = env["eps"]

    wfs = ent(nc.sbuf_tensor("wfs", [128, 4 * 2048], F32))
    wqb = ent(nc.sbuf_tensor("wqb", [128, 4 * 2048], BF16))
    wqT = ent(nc.sbuf_tensor("wqT", [128, 16 * 512], BF16))
    xqs = ent(nc.sbuf_tensor("xqs", [128, 4 * 2048], BF16))
    xqT = ent(nc.sbuf_tensor("xqT", [128, 2 * 16 * 512], BF16))
    o1f = ent(nc.sbuf_tensor("o1f", [128, 1], F32))
    orf = ent(nc.sbuf_tensor("orf", [1, 128], F32))
    idb = ent(nc.sbuf_tensor("idb", [128, 128], BF16))
    tabs = ent(nc.sbuf_tensor("tabs", [128, 1], F32))
    asum = ent(nc.sbuf_tensor("asum", [128, 2], F32))
    af = ent(nc.sbuf_tensor("af", [1, 2], F32))
    ab = ent(nc.sbuf_tensor("ab", [128, 1], F32))
    rba = ent(nc.sbuf_tensor("rba", [128, 1], F32))
    rmaxb = ent(nc.sbuf_tensor("rmaxb", [128, 1], F32))
    betab = ent(nc.sbuf_tensor("betab", [128, 1], F32))
    rbb = ent(nc.sbuf_tensor("rbb", [128, 1], F32))
    bab = ent(nc.sbuf_tensor("bab", [128, 8], F32))
    psQ1 = ent(nc.psum_tensor("psQ1", [128, 128], BF16))
    psQ2 = ent(nc.psum_tensor("psQ2", [128, 128], BF16))
    psSm = ent(nc.psum_tensor("psSm", [128, 512], F32))

    T0, V0, A0 = cnt["t"], cnt["v"], cnt["a"]
    li0 = gld["i"]
    with nc.Block() as blk:
        @blk.sync
        def _(e):
            e.dma_start(out=o1f[:, :], in_=o1f_d[:, :]).then_inc(dX, 16)
            e.dma_start(out=orf[:, :], in_=orf_d[:, :]).then_inc(dX, 16)
            e.dma_start(out=idb[:, :], in_=idb_d[:, :]).then_inc(dX, 16)
        cnt["dx"] += 3

        @blk.gpsimd
        def _(e):
            for st in range(16):
                if st >= 2:
                    e.wait_ge(aa, A0 + st - 1)
                g_load(e, wfs[:, (st % 2) * 2048:(st % 2) * 2048 + 2048],
                       wq_d[st * 128:(st + 1) * 128, :])

        @blk.scalar
        def _(e):
            for st in range(16):
                e.wait_ge(aa, A0 + st)
                g_wait(e, li0 + st)
                if st >= 1:
                    e.wait_ge(vv, V0 + 1 + st)
                e.activation(junk[:, :], wfs[:, (st % 2) * 2048:(st % 2) * 2048 + 2048],
                             AF.Abs, accum_out=tabs[:, :]).then_inc(aa, 1)
            cnt["a"] = A0 + 16

        @blk.vector
        def _(e):
            nv = V0
            e.wait_ge(vv, nv)
            e.memset(asum[:, :], 0.0).then_inc(vv, 1); nv += 1
            for st in range(16):
                e.wait_ge(aa, A0 + st + 1)
                e.wait_ge(vv, nv)
                e.tensor_tensor(asum[:, 0:1], tabs[:, :], asum[:, 0:1], ALU.add).then_inc(vv, 1)
                nv += 1
            cnt["v"] = nv
    cnt["t"] = T0

    T0, V0, A0 = cnt["t"], cnt["v"], cnt["a"]
    NDX = cnt["dx"]
    with nc.Block() as blk:
        @blk.tensor
        def _(e):
            e.wait_ge(dX, 16 * NDX)
            e.matmul(psSm[0:1, 0:2], o1f[:, :], asum[:, 0:2], start=True, stop=True).then_inc(tMM, 1)
            e.wait_ge(vv, V0 + 2)
            e.matmul(psSm[:, 0:2], orf[:, :], af[:, :], start=True, stop=True).then_inc(tMM, 1)

        @blk.vector
        def _(e):
            nv = V0
            e.wait_ge(tMM, T0 + 1)
            e.wait_ge(vv, nv)
            e.tensor_scalar(af[:, :], psSm[0:1, 0:2], 1.0 / 4194304.0, 1e-8,
                            ALU.mult, ALU.max).then_inc(vv, 1); nv += 1
            e.wait_ge(vv, nv)
            e.tensor_copy(af[:, :], af[:, :]).then_inc(vv, 1); nv += 1
            e.wait_ge(tMM, T0 + 2)
            e.wait_ge(vv, nv)
            e.tensor_copy(ab[:, :], psSm[:, 0:1]).then_inc(vv, 1); nv += 1
            e.wait_ge(vv, nv)
            e.reciprocal(rba[:, :], ab[:, :]).then_inc(vv, 1); nv += 1
            cnt["v"] = nv
    cnt["t"] = T0 + 2

    T0, V0, A0 = cnt["t"], cnt["v"], cnt["a"]
    with nc.Block() as blk:
        @blk.tensor
        def _(e):
            for i in range(128):
                t8, ic = divmod(i, 16)
                if i >= 2:
                    e.wait_ge(aa, A0 + i - 1)
                e.wait_ge(vv, V0 + 6 * t8 + 5)
                e.transpose((psQ1 if i % 2 == 0 else psQ2)[:, :],
                            xqs[:, (t8 % 4) * 2048 + ic * 128: (t8 % 4) * 2048 + (ic + 1) * 128],
                            idb[:, :]).then_inc(tMM, 1)

        @blk.scalar
        def _(e):
            for i in range(128):
                t8, ic = divmod(i, 16)
                th, tl = divmod(t8, 4)
                e.wait_ge(tMM, T0 + i + 1)
                e.activation(xqT[:, th * 8192 + ic * 512 + tl * 128:
                                th * 8192 + ic * 512 + tl * 128 + 128],
                             (psQ1 if i % 2 == 0 else psQ2)[:, :],
                             AF.Identity).then_inc(aa, 1)
            cnt["a"] = A0 + 128

        @blk.vector
        def _(e):
            nv = V0
            for t8 in range(8):
                xt = xsb[:, t8 * D:(t8 + 1) * D]
                if t8 >= 4:
                    e.wait_ge(tMM, T0 + 16 * (t8 - 3))
                e.wait_ge(vv, nv)
                e.reduce_max(rmaxb[:, :], xt, axis=AX.X, apply_absolute_value=True).then_inc(vv, 1); nv += 1
                e.wait_ge(vv, nv)
                e.tensor_scalar(betab[:, :], rmaxb[:, :], 1.0 / 127.0, 1e-8,
                                ALU.mult, ALU.max).then_inc(vv, 1); nv += 1
                e.wait_ge(vv, nv)
                e.reciprocal(rbb[:, :], betab[:, :]).then_inc(vv, 1); nv += 1
                e.wait_ge(vv, nv)
                e.tensor_scalar(junk[:, :], xt, rbb[:, :], RC, ALU.mult, ALU.add).then_inc(vv, 1); nv += 1
                e.wait_ge(vv, nv)
                e.tensor_scalar(xqs[:, (t8 % 4) * 2048:((t8 % 4) + 1) * 2048], junk[:, :], RC, None,
                                ALU.subtract, ALU.bypass).then_inc(vv, 1); nv += 1
                e.wait_ge(vv, nv)
                e.tensor_tensor(bab[:, t8: t8 + 1], ab[:, :], betab[:, :], ALU.mult).then_inc(vv, 1); nv += 1
            cnt["v"] = nv
    cnt["t"] = T0 + 128

    for ocg in range(4):
        T0, V0, A0 = cnt["t"], cnt["v"], cnt["a"]
        li0 = gld["i"]
        with nc.Block() as blk:
            @blk.gpsimd
            def _(e):
                g_load(e, wfs[:, :].rearrange("p (oc i) -> p oc i", i=D),
                       wq_d[ocg * 512:(ocg + 1) * 512, :].rearrange("(oc p) i -> p oc i", p=128))

            @blk.vector
            def _(e):
                nv = V0
                g_wait(e, li0)
                for oc in range(4):
                    wf = wfs[:, oc * 2048:(oc + 1) * 2048]
                    wq = wqb[:, oc * 2048:(oc + 1) * 2048]
                    e.wait_ge(vv, nv)
                    e.tensor_scalar(junk[:, :], wf, rba[:, :], RC, ALU.mult, ALU.add).then_inc(vv, 1); nv += 1
                    e.wait_ge(vv, nv)
                    e.tensor_scalar(junk[:, :], junk[:, :], RC, -1.0, ALU.subtract, ALU.max).then_inc(vv, 1); nv += 1
                    e.wait_ge(vv, nv)
                    e.tensor_scalar(wq, junk[:, :], 1.0, None, ALU.min, ALU.bypass).then_inc(vv, 1); nv += 1
                for tl in range(8):
                    e.wait_ge(tMM, T0 + 64 + tl + 1)
                    xsl = xsb[:, tl * D + ocg * 512: tl * D + ocg * 512 + 512]
                    e.wait_ge(vv, nv)
                    e.scalar_tensor_tensor(xsl, psA[:, (tl % 4) * 512:((tl % 4) + 1) * 512],
                                           bab[:, tl: tl + 1], xsl,
                                           ALU.mult, ALU.add).then_inc(vv, 1)
                    nv += 1
                cnt["v"] = nv

            @blk.scalar
            def _(e):
                for i in range(64):
                    oc, ic = divmod(i, 16)
                    e.wait_ge(tMM, T0 + i + 1)
                    e.activation(wqT[:, ic * 512 + oc * 128: ic * 512 + oc * 128 + 128],
                                 (psQ1 if i % 2 == 0 else psQ2)[:, :],
                                 AF.Identity).then_inc(aa, 1)
                cnt["a"] = A0 + 64

            @blk.tensor
            def _(e):
                for i in range(64):
                    oc, ic = divmod(i, 16)
                    e.wait_ge(vv, V0 + 3 * (oc + 1))
                    if i >= 2:
                        e.wait_ge(aa, A0 + i - 1)
                    e.transpose((psQ1 if i % 2 == 0 else psQ2)[:, :],
                                wqb[:, oc * 2048 + ic * 128: oc * 2048 + ic * 128 + 128],
                                idb[:, :]).then_inc(tMM, 1)
                for tl in range(8):
                    e.wait_ge(aa, A0 + 64)
                    if tl >= 4:
                        e.wait_ge(vv, V0 + 12 + (tl - 4) + 1)
                    for ic in range(16):
                        mm = e.matmul(psA[:, (tl % 4) * 512:((tl % 4) + 1) * 512],
                                      xqT[:, (tl // 4) * 8192 + ic * 512 + (tl % 4) * 128:
                                          (tl // 4) * 8192 + ic * 512 + (tl % 4) * 128 + 128],
                                      wqT[:, ic * 512: ic * 512 + 512],
                                      start=(ic == 0), stop=(ic == 15))
                        if ic == 15:
                            mm.then_inc(tMM, 1)
        cnt["t"] = T0 + 72


def build_post(layer):
    nc = bass.Bass("TRN2", target_bir_lowering=False)
    io = {}
    io["x_d"] = nc.dram_tensor("x", [TPC, D], F32, kind="ExternalInput")
    io["cxo_d"] = nc.dram_tensor("ctxo", [D, TPC], BF16, kind="ExternalInput")
    io["wo_d"] = nc.dram_tensor("wout", [D, D], F32, kind="ExternalInput")
    io["id_d"] = nc.dram_tensor("ident32", [128, 128], F32, kind="ExternalInput")
    if layer == 0:
        io["bo_d"] = nc.dram_tensor("boutbc", [128, D], F32, kind="ExternalInput")
        io["wg_d"] = nc.dram_tensor("wgate", [D, 8], F32, kind="ExternalInput")
        io["wgu_d"] = nc.dram_tensor("wgu", [8, D, D], F32, kind="ExternalInput")
        io["wdn_d"] = nc.dram_tensor("wdn", [8, 1024, D], F32, kind="ExternalInput")
        io["wsgu_d"] = nc.dram_tensor("wsgu", [D, 8192], F32, kind="ExternalInput")
        io["wsdn_d"] = nc.dram_tensor("wsdn", [4096, D], F32, kind="ExternalInput")
        io["sel_d"] = nc.dram_tensor("sel", [8, 1024], BF16, kind="ExternalInput")
    else:
        io["wq_d"] = nc.dram_tensor("wq", [D, D], F32, kind="ExternalInput")
        io["o1f_d"] = nc.dram_tensor("ones128f", [128, 1], F32, kind="ExternalInput")
        io["orf_d"] = nc.dram_tensor("onesrowf", [1, 128], F32, kind="ExternalInput")
        io["idb_d"] = nc.dram_tensor("identb", [128, 128], BF16, kind="ExternalInput")
    io["xo_d"] = nc.dram_tensor("xo", [TPC, D], F32, kind="ExternalOutput")
    with contextlib.ExitStack() as ctx:
        xsb = ctx.enter_context(nc.sbuf_tensor("xsb", [128, 8 * D], F32))
        emit_post(nc, layer, xsb, io, ctx.enter_context, load_x=True, store=True)
    return nc


def emit_post(nc, layer, xsb, io, sement, load_x, store, coll=None):
    x_d = io.get("x_d")
    wo_d, id_d = io["wo_d"], io["id_d"]
    if layer == 0:
        bo_d = io["bo_d"]
        wg_d, wgu_d, wdn_d = io["wg_d"], io["wgu_d"], io["wdn_d"]
        wsgu_d, wsdn_d, sel_d = io["wsgu_d"], io["wsdn_d"], io["sel_d"]
    else:
        wq_d = io["wq_d"]
        o1f_d, orf_d, idb_d = io["o1f_d"], io["orf_d"], io["idb_d"]
    xo_d = io.get("xo_d")

    EV = 2 if layer == 0 else 1

    with contextlib.ExitStack() as ctx:
        ent = ctx.enter_context

        junk = ent(nc.sbuf_tensor("junk", [128, 2048], F32))
        ids32 = ent(nc.sbuf_tensor("ids32", [128, 128], F32))
        eps = ent(nc.sbuf_tensor("eps", [128, 1], F32))
        srow = ent(nc.sbuf_tensor("srow", [128, 1], F32))
        sqs = ent(nc.sbuf_tensor("sqs", [128, 1], F32))
        mu = ent(nc.sbuf_tensor("mu", [128, 1], F32))
        msq = ent(nc.sbuf_tensor("msq", [128, 1], F32))
        e2b = ent(nc.sbuf_tensor("e2b", [128, 1], F32))
        varb = ent(nc.sbuf_tensor("varb", [128, 1], F32))
        lnvb = ent(nc.sbuf_tensor("lnvb", [128, 1], F32))
        rstd = ent(nc.sbuf_tensor("rstd", [128, 1], F32))
        mrs = ent(nc.sbuf_tensor("mrs", [128, 1], F32))
        dX = sement(nc.semaphore("dX"))
        dA0 = sement(nc.semaphore("dA0"))
        dA1 = sement(nc.semaphore("dA1"))
        dB = [sement(nc.semaphore("dB%d" % i)) for i in range(4)]
        dCX = sement(nc.semaphore("dCX"))
        dS = sement(nc.semaphore("dS"))
        tMM = sement(nc.semaphore("tMM"))
        aa = sement(nc.semaphore("aa"))
        vv = sement(nc.semaphore("vv"))
        cnt = {"t": 0, "a": 0, "v": 0}
        cnt["dx"] = 0
        opctx = contextlib.ExitStack()
        cxs = opctx.enter_context(nc.sbuf_tensor("cxs", [128, 16 * TPC], BF16))
        wsl = opctx.enter_context(nc.sbuf_tensor("wsl", [128, 2 * 16 * 512], BF16))
        scr = opctx.enter_context(nc.sbuf_tensor("scr", [128, 512], F32))
        if layer == 0:
            bo = opctx.enter_context(nc.sbuf_tensor("bo", [128, D], F32))
        gld = {"i": 0, "n": [0, 0, 0, 0]}

        def g_load(e, dst_ap, src_ap):
            p = gld["i"] % 4
            gld["i"] += 1
            if gld["n"][p] >= 1:
                e.wait_ge(dB[p], 16 * gld["n"][p])
            gld["n"][p] += 1
            e.dma_start(out=dst_ap, in_=src_ap).then_inc(dB[p], 16)

        def g_wait(e, li):
            e.wait_ge(dB[li % 4], 16 * (li // 4 + 1))

        with nc.Block() as blk:
            @blk.sync
            def _(e):
                if load_x:
                    e.dma_start(out=xsb[:, :].rearrange("p (tt d) -> p tt d", d=D),
                                in_=x_d[:, :].rearrange("(tt p) d -> p tt d", p=128)).then_inc(dX, 16)
                cxo_d = io["cxo_d"]
                if coll is not None:
                    e.wait_ge(coll["cc"], coll["base"] + 4)
                e.dma_start(out=cxs[:, :].rearrange("p (q r t) -> p q r t", r=2, t=TPC),
                            in_=cxo_d[:, :].rearrange("(q r p) t -> p q r t", r=2, p=128)
                            ).then_inc(dX, 16)
                e.dma_start(out=ids32[:, :], in_=id_d[:, :]).then_inc(dX, 16)
                if layer == 0:
                    e.dma_start(out=bo[:, :], in_=bo_d[:, :]).then_inc(dX, 16)

            @blk.gpsimd
            def _(e):
                for sl in range(2):
                    src = wo_d[:, sl * 512:(sl + 1) * 512].rearrange("(kc p) n -> p kc n", p=128)
                    e.dma_start(out=wsl[:, sl * 8192: sl * 8192 + 8192].rearrange("p (kc n) -> p kc n", n=512),
                                in_=src).then_inc(dA0 if sl % 2 == 0 else dA1, 16)

            @blk.vector
            def _(e):
                e.memset(eps[:, :], 1e-5)

        NPRE = (1 if load_x else 0) + 2 + (1 if layer == 0 else 0)
        cnt["dx"] = NPRE

        psA = ent(nc.psum_tensor("psA", [128, 2048], F32))
        with nc.Block() as blk:
            @blk.gpsimd
            def _(e):
                for sl in range(2, 4):
                    e.wait_ge(tMM, 8 * (sl - 1))
                    src = wo_d[:, sl * 512:(sl + 1) * 512].rearrange("(kc p) n -> p kc n", p=128)
                    e.dma_start(out=wsl[:, (sl % 2) * 8192:(sl % 2) * 8192 + 8192].rearrange("p (kc n) -> p kc n", n=512),
                                in_=src).then_inc(dA0 if sl % 2 == 0 else dA1, 16)

            @blk.tensor
            def _(e):
                e.wait_ge(dX, 16 * NPRE)
                for ocg in range(4):
                    e.wait_ge(dA0 if ocg % 2 == 0 else dA1, 16 * (ocg // 2 + 1))
                    for tt in range(8):
                        idx = ocg * 8 + tt
                        if idx >= 4:
                            e.wait_ge(vv, EV * (idx - 3))
                        for hc in range(16):
                            mm = e.matmul(psA[:, (idx % 4) * 512:(idx % 4) * 512 + 512],
                                          cxs[:, hc * TPC + tt * 128: hc * TPC + tt * 128 + 128],
                                          wsl[:, (ocg % 2) * 8192 + hc * 512:(ocg % 2) * 8192 + hc * 512 + 512],
                                          start=(hc == 0), stop=(hc == 15))
                            if hc == 15:
                                mm.then_inc(tMM, 1)

            @blk.vector
            def _(e):
                nv = 0
                for ocg in range(4):
                    for tt in range(8):
                        idx = ocg * 8 + tt
                        e.wait_ge(tMM, idx + 1)
                        xsl = xsb[:, tt * D + ocg * 512: tt * D + ocg * 512 + 512]
                        ps = psA[:, (idx % 4) * 512:(idx % 4) * 512 + 512]
                        if layer == 0:
                            e.wait_ge(vv, nv)
                            e.tensor_tensor(scr[:, :], ps, bo[:, ocg * 512:(ocg + 1) * 512],
                                            ALU.add).then_inc(vv, 1); nv += 1
                            e.wait_ge(vv, nv)
                            e.tensor_tensor(xsl, scr[:, :], xsl, ALU.add).then_inc(vv, 1); nv += 1
                        else:
                            e.wait_ge(vv, nv)
                            e.scalar_tensor_tensor(xsl, ps, 1.0, xsl,
                                                   ALU.mult, ALU.add).then_inc(vv, 1); nv += 1
                cnt["v"] = nv
        cnt["t"] = 32
        opctx.close()

        def ln_phase():
            V0, A0 = cnt["v"], cnt["a"]
            with nc.Block() as blk:
                @blk.scalar
                def _(e):
                    na = A0
                    for i in range(8):
                        e.wait_ge(aa, na)
                        if i >= 1:
                            e.wait_ge(vv, V0 + 7 * (i - 1) + 4)
                        e.activation(junk[:, :], xsb[:, i * D:(i + 1) * D], AF.Square,
                                     accum_out=sqs[:, :]).then_inc(aa, 1); na += 1
                        e.wait_ge(aa, na)
                        e.wait_ge(vv, V0 + 7 * i + 5)
                        e.activation(lnvb[:, :], varb[:, :], AF.Ln, bias=eps[:, :]).then_inc(aa, 1); na += 1
                        e.wait_ge(aa, na)
                        e.activation(rstd[:, :], lnvb[:, :], AF.Exp, scale=-0.5).then_inc(aa, 1); na += 1
                    cnt["a"] = na

                @blk.vector
                def _(e):
                    nv = V0

                    def vop(inst):
                        nonlocal nv
                        inst.then_inc(vv, 1); nv += 1

                    for i in range(8):
                        xt = xsb[:, i * D:(i + 1) * D]
                        e.wait_ge(vv, nv); vop(e.reduce_sum(srow[:, :], xt, axis=AX.X))
                        e.wait_ge(vv, nv); vop(e.tensor_scalar(mu[:, :], srow[:, :], 1.0 / 2048.0, None, ALU.mult, ALU.bypass))
                        e.wait_ge(vv, nv); vop(e.tensor_tensor(msq[:, :], mu[:, :], mu[:, :], ALU.mult))
                        e.wait_ge(vv, nv); e.wait_ge(aa, A0 + 3 * i + 1)
                        vop(e.tensor_scalar(e2b[:, :], sqs[:, :], 1.0 / 2048.0, None, ALU.mult, ALU.bypass))
                        e.wait_ge(vv, nv); vop(e.tensor_tensor(varb[:, :], e2b[:, :], msq[:, :], ALU.subtract))
                        e.wait_ge(vv, nv); e.wait_ge(aa, A0 + 3 * i + 3)
                        vop(e.tensor_tensor(mrs[:, :], mu[:, :], rstd[:, :], ALU.mult))
                        e.wait_ge(vv, nv); vop(e.tensor_scalar(xt, xt, rstd[:, :], mrs[:, :], ALU.mult, ALU.subtract))
                    cnt["v"] = nv

        ln_phase()

        if layer == 0:
            emit_moe(nc, ctx, cnt, dict(xsb=xsb, junk=junk, ids32=ids32, eps=eps,
                                        wg_d=wg_d, wgu_d=wgu_d, wdn_d=wdn_d,
                                        wsgu_d=wsgu_d, wsdn_d=wsdn_d, sel_d=sel_d,
                                        psA=psA, tMM=tMM, aa=aa, vv=vv, dX=dX,
                                        g_load=g_load, g_wait=g_wait, gld=gld, ent=ent))
        else:
            emit_bitnet(nc, ctx, cnt, dict(xsb=xsb, junk=junk, eps=eps,
                                           wq_d=wq_d, o1f_d=o1f_d, orf_d=orf_d, idb_d=idb_d,
                                           psA=psA, tMM=tMM, aa=aa, vv=vv, dX=dX,
                                           g_load=g_load, g_wait=g_wait, gld=gld, ent=ent))

        ln_phase()

        if store:
            with nc.Block() as blk:
                @blk.sync
                def _(e):
                    e.dma_start(out=xo_d[:, :].rearrange("(tt p) d -> p tt d", p=128),
                                in_=xsb[:, :].rearrange("p (tt d) -> p tt d", d=D)).then_inc(dS, 16)
                    e.wait_ge(dS, 16)


class _NcProxy:

    def __init__(self, nc):
        object.__setattr__(self, "_nc", nc)
        object.__setattr__(self, "_ctr", [0])

    def _u(self, name):
        self._ctr[0] += 1
        return "%s_u%d" % (name, self._ctr[0])

    def sbuf_tensor(self, name, *a, **k):
        return self._nc.sbuf_tensor(self._u(name), *a, **k)

    def psum_tensor(self, name, *a, **k):
        return self._nc.psum_tensor(self._u(name), *a, **k)

    def semaphore(self, name=None, num=None):
        return self._nc.semaphore(self._u(name or "s"), num)

    def __getattr__(self, a):
        return getattr(self._nc, a)


def build_fused():
    nc = bass.Bass("TRN2", target_bir_lowering=False)
    T = nc.dram_tensor
    x_d = T("x", [TPC, D], F32, kind="ExternalInput")
    y_d = T("y", [TPC, D], F32, kind="ExternalOutput")
    id_d = T("ident32", [128, 128], F32, kind="ExternalInput")
    idb_d = T("identb", [128, 128], BF16, kind="ExternalInput")
    ra_d = T("rope_a", [128, 4096], BF16, kind="ExternalInput")
    rb_d = T("rope_b", [128, 4096], BF16, kind="ExternalInput")
    mk_d = T("mask", [128, 896], BF16, kind="ExternalInput")
    iv_d = T("inv128", [128, 1], BF16, kind="ExternalInput")
    or_d = T("onesrow", [1, 128], BF16, kind="ExternalInput")
    o1f_d = T("ones128f", [128, 1], F32, kind="ExternalInput")
    orf_d = T("onesrowf", [1, 128], F32, kind="ExternalInput")
    sel_d = T("sel", [8, 1024], BF16, kind="ExternalInput")
    w0qkv_d = T("w0qkv", [D, 3 * D], F32, kind="ExternalInput")
    bq0_d = T("bqkv128", [128, 48], F32, kind="ExternalInput")
    bv0_d = T("bvbc", [128, D], F32, kind="ExternalInput")
    w0o_d = T("w0out", [D, D], F32, kind="ExternalInput")
    bo0_d = T("boutbc", [128, D], F32, kind="ExternalInput")
    wg_d = T("wgate", [D, 8], F32, kind="ExternalInput")
    wgu_d = T("wgu", [8, D, D], F32, kind="ExternalInput")
    wdn_d = T("wdn", [8, 1024, D], F32, kind="ExternalInput")
    wsgu_d = T("wsgu", [D, 8192], F32, kind="ExternalInput")
    wsdn_d = T("wsdn", [4096, D], F32, kind="ExternalInput")
    w1qkv_d = T("w1qkv", [D, 3 * D], F32, kind="ExternalInput")
    ls_d = T("lnsc", [128, 1], F32, kind="ExternalInput")
    lb_d = T("lnbi", [128, 1], F32, kind="ExternalInput")
    w1o_d = T("w1out", [D, D], F32, kind="ExternalInput")
    wq_d = T("wq", [D, D], F32, kind="ExternalInput")
    q_b = T("q_b", [2048, TPC], BF16)
    q_a = T("q_a", [2048, TPC], BF16)
    k_b = T("k_b", [2048, TPC], BF16)
    k_a = T("k_a", [2048, TPC], BF16)
    v_b = T("v_b", [8 * TPC, 256], BF16)
    v_g = T("v_g", [8 * TPC, 256], BF16)
    ctxa_in = T("ctxa_in", [2048, TPC], BF16)
    ctx_out = T("ctx_out", [2048, TPC], BF16)

    ncp = _NcProxy(nc)
    with contextlib.ExitStack() as ctx:
        ent = ctx.enter_context
        xsb = ent(nc.sbuf_tensor("xsb", [128, 8 * D], F32))
        dX0 = ent(nc.semaphore("dX0"))

        with nc.Block() as blk:
            @blk.sync
            def _(e):
                e.dma_start(out=xsb[:, :].rearrange("p (tt d) -> p tt d", d=D),
                            in_=x_d[:, :].rearrange("(tt p) d -> p tt d", p=128)).then_inc(dX0, 16)
                e.wait_ge(dX0, 16)

        cc = ent(nc.semaphore("cc"))
        ncc = 0
        for layer in range(2):
            mio = {"w_d": w0qkv_d if layer == 0 else w1qkv_d,
                   "ra_d": ra_d, "rb_d": rb_d, "id_d": id_d,
                   "qT_d": q_b, "kT_d": k_b, "v_d": v_b}
            if layer == 0:
                mio.update(bq_d=bq0_d, bv_d=bv0_d)
            else:
                mio.update(iv_d=iv_d, or_d=or_d, ls_d=ls_d, lb_d=lb_d)
            coll = {"cc": cc, "base": ncc, "q_in": q_b, "q_out": q_a,
                    "k_in": k_b, "k_out": k_a,
                    "v_in": v_b, "v_out": v_g, "ctx_in": ctxa_in, "ctx_out": ctx_out}
            emit_mixer(ncp, layer, xsb, mio, ent, load_x=False, coll=coll)

            aio = {"mk_d": mk_d, "id_d": id_d, "ctxa_d": ctxa_in,
                   "qa_d": q_a, "ka_d": k_a, "vg_d": v_g}
            emit_attn(ncp, layer, aio, ent, coll=coll)

            pio = {"wo_d": w0o_d if layer == 0 else w1o_d, "id_d": id_d,
                   "cxo_d": ctx_out, "xo_d": y_d}
            if layer == 0:
                pio.update(bo_d=bo0_d, wg_d=wg_d, wgu_d=wgu_d, wdn_d=wdn_d,
                           wsgu_d=wsgu_d, wsdn_d=wsdn_d, sel_d=sel_d)
            else:
                pio.update(wq_d=wq_d, o1f_d=o1f_d, orf_d=orf_d, idb_d=idb_d)
            emit_post(ncp, layer, xsb, pio, ent, load_x=False, store=(layer == 1), coll=coll)
            ncc += 4

    return nc


_PROG_CACHE = {}


def _prog(name, builder, *args):
    if name not in _PROG_CACHE:
        _PROG_CACHE[name] = builder(*args)
    return _PROG_CACHE[name]


def kernel(**inputs):
    z = inputs
    f64 = np.asarray(z["L0_attn_x"]).dtype == np.float64
    con = host_prep(f64)
    x_full = np.asarray(z["L0_attn_x"]).astype(np.float32, copy=False).reshape(2 * S, D)
    bq0 = np.asarray(z["L0_attn_bqkv"]).astype(np.float32, copy=False)

    nc = _prog("fused", build_fused)
    maps = []
    asf = lambda k: np.asarray(z[k]).astype(np.float32, copy=False)
    for core in range(8):
        ra, rb = rope_rep_for_core(con, core)
        m = {
            "x": np.ascontiguousarray(x_full[core * TPC:(core + 1) * TPC]),
            "ident32": con["ident32"], "identb": con["ident"],
            "rope_a": ra, "rope_b": rb, "mask": con["mask"],
            "inv128": con["inv128"], "onesrow": con["onesrow"],
            "ones128f": np.ones((128, 1), np.float32),
            "onesrowf": np.ones((1, 128), np.float32),
            "sel": con["sel"],
            "w0qkv": asf("L0_attn_Wqkv"),
            "bqkv128": np.ascontiguousarray(bq0.reshape(48, 128).T),
            "bvbc": np.ascontiguousarray(np.broadcast_to(bq0[4096:], (128, D))),
            "w0out": asf("L0_attn_Wout"),
            "boutbc": np.ascontiguousarray(np.broadcast_to(asf("L0_attn_bout"), (128, D))),
            "wgate": asf("L0_ffn_w_gate"), "wgu": asf("L0_ffn_w_gate_up"),
            "wdn": asf("L0_ffn_w_down"), "wsgu": asf("L0_ffn_w_sh_gate_up"),
            "wsdn": asf("L0_ffn_w_sh_down"),
            "w1qkv": asf("L1_attn_w_qkv"),
            "lnsc": np.ascontiguousarray(asf("L1_attn_ln_scale")[:, None]),
            "lnbi": np.ascontiguousarray(asf("L1_attn_ln_bias")[:, None]),
            "w1out": asf("L1_attn_w_out"),
            "wq": asf("L1_ffn_weight"),
        }
        maps.append(m)
    res = run_bass_kernel_spmd(nc, maps, core_ids=list(range(8)))
    y = np.concatenate([res.results[c]["y"] for c in range(8)], axis=0)
    return y.reshape(2, S, D).astype(np.float32, copy=False)


def kernel_unfused(**inputs):
    z = inputs
    f64 = np.asarray(z["L0_attn_x"]).dtype == np.float64
    con = host_prep(f64)
    x_full = np.asarray(z["L0_attn_x"]).astype(np.float32, copy=False).reshape(2 * S, D)

    def run(nc, maps):
        return run_bass_kernel_spmd(nc, maps, core_ids=list(range(8)))

    bq0 = np.asarray(z["L0_attn_bqkv"]).astype(np.float32, copy=False)

    out = np.empty((2 * S, D), np.float32)
    xcur = x_full
    for layer in range(2):
        wqkv = np.asarray(z["L0_attn_Wqkv"] if layer == 0 else z["L1_attn_w_qkv"]).astype(np.float32, copy=False)
        wout = np.asarray(z["L0_attn_Wout"] if layer == 0 else z["L1_attn_w_out"]).astype(np.float32, copy=False)

        nc = _prog("mix%d" % layer, build_mixer, layer)
        maps = []
        for core in range(8):
            ra, rb = rope_rep_for_core(con, core)
            m = {"x": np.ascontiguousarray(xcur[core * TPC:(core + 1) * TPC]),
                 "wqkv": wqkv, "rope_a": ra, "rope_b": rb, "ident32": con["ident32"]}
            if layer == 0:
                m["bqkv128"] = np.ascontiguousarray(bq0.reshape(48, 128).T)
                m["bvbc"] = np.ascontiguousarray(np.broadcast_to(bq0[4096:], (128, D)))
            else:
                m["inv128"] = con["inv128"]
                m["onesrow"] = con["onesrow"]
                m["lnsc"] = np.ascontiguousarray(np.asarray(z["L1_attn_ln_scale"]).astype(np.float32, copy=False)[:, None])
                m["lnbi"] = np.ascontiguousarray(np.asarray(z["L1_attn_ln_bias"]).astype(np.float32, copy=False)[:, None])
            maps.append(m)
        res = run(nc, maps)
        qk_shard = [res.results[c]["qkT"] for c in range(8)]
        v_full = np.concatenate([res.results[c]["vout"] for c in range(8)], axis=0)

        nc = _prog("att%d" % layer, build_attn, layer)
        maps = []
        for core in range(8):
            qka = np.concatenate([qk_shard[i][core * 512:(core + 1) * 512] for i in range(8)], axis=0)
            maps.append({"qka": np.ascontiguousarray(qka), "vg": v_full,
                         "mask": con["mask"], "ident32": con["ident32"]})
        res = run(nc, maps)
        ctx_shard = [res.results[c]["ctxa"] for c in range(8)]

        nc = _prog("post%d" % layer, build_post, layer)
        maps = []
        for core in range(8):
            cxo = np.concatenate([ctx_shard[i][core * 256:(core + 1) * 256] for i in range(8)], axis=0)
            m = {"x": np.ascontiguousarray(xcur[core * TPC:(core + 1) * TPC]),
                 "ctxo": np.ascontiguousarray(cxo), "wout": wout, "ident32": con["ident32"]}
            if layer == 0:
                m["boutbc"] = np.ascontiguousarray(np.broadcast_to(
                    np.asarray(z["L0_attn_bout"]).astype(np.float32, copy=False), (128, D)))
                m["wgate"] = np.asarray(z["L0_ffn_w_gate"]).astype(np.float32, copy=False)
                m["wgu"] = np.asarray(z["L0_ffn_w_gate_up"]).astype(np.float32, copy=False)
                m["wdn"] = np.asarray(z["L0_ffn_w_down"]).astype(np.float32, copy=False)
                m["wsgu"] = np.asarray(z["L0_ffn_w_sh_gate_up"]).astype(np.float32, copy=False)
                m["wsdn"] = np.asarray(z["L0_ffn_w_sh_down"]).astype(np.float32, copy=False)
                m["sel"] = con["sel"]
            else:
                m["wq"] = np.asarray(z["L1_ffn_weight"]).astype(np.float32, copy=False)
                m["ones128f"] = np.ones((128, 1), np.float32)
                m["onesrowf"] = np.ones((1, 128), np.float32)
                m["identb"] = con["ident"]
            maps.append(m)
        res = run(nc, maps)
        xcur = np.concatenate([res.results[c]["xo"] for c in range(8)], axis=0)

    return xcur.reshape(2, S, D).astype(np.float32, copy=False)
